# version 03 (final)
# baseline (speedup 1.0000x reference)
import math
import re
from contextlib import ExitStack

import numpy as np
import ml_dtypes

import concourse.bass as bass
import bass_rust as _br
import concourse.mybir as mybir
from concourse.bass_utils import run_bass_kernel_spmd

F32 = mybir.dt.float32
BF16 = mybir.dt.bfloat16
I32 = mybir.dt.int32
AF = mybir.ActivationFunctionType
ALU = mybir.AluOpType
BF_NP = ml_dtypes.bfloat16

EPS = 1e-5


class G:
    def __init__(self, dm, S, TOK, VE, TE):
        self.dm = dm
        self.S = S
        self.TOK = TOK
        self.VE = VE
        self.TS = VE
        self.TE = TE
        self.DC = dm // 128
        self.H = dm // 128
        self.HO = self.H // 2
        self.T = S // 2
        self.W = min(2 * TOK, self.T)
        self.NTW = self.W // TOK
        self.NHP = self.T // self.W
        self.NBLK = S // self.W
        self.NVIS = VE - 1
        self.NKEEP = (VE - 1) // 2
        self.NG = S // 512
        self.SCALE = 1.0 / math.sqrt(128.0)
        assert self.W % TOK == 0 and S % 512 == 0 and dm % 128 == 0
        assert self.NTW % 2 == 0


FULL = G(4096, 4096, 512, 513, 4040)
MINI = G(512, 1024, 256, 65, 968)


class Build0:
    pass


class Build:
    def __init__(self, g, n_phases=99, solo=False):
        self.g = g
        self.solo = solo
        self.n_phases = n_phases
        self.nc = bass.Bass("TRN2", target_bir_lowering=False, debug=False)
        self.es = ExitStack()
        self._mk_dram()
        self._mk_const()
        self._mk_persistent_sbuf()
        self.phase_idx = 0
        self.nsems = 0
        self.last_store = None
        self.rankreg = {}
        for key, et in [("s", mybir.EngineType.SP),
                        ("v", mybir.EngineType.DVE),
                        ("a", mybir.EngineType.Activation),
                        ("t", mybir.EngineType.PE),
                        ("g", mybir.EngineType.Pool)]:
            self.rankreg[key] = self.nc.alloc_register(et, f"rk_{key}")
        self._freed_regs = set()

    def free_temps(self, exclude=()):
        import io
        buf = io.StringIO()
        self.nc.print_concise(file=buf)
        names = set()
        for m in re.finditer(r"R(?:64)?\[([A-Za-z0-9_:]+)\]", buf.getvalue()):
            for part in m.group(1).split(":"):
                names.add(part)
        for nm in sorted(names):
            if nm in self._freed_regs or nm in exclude:
                continue
            body = nm.split("_", 1)[1] if "_" in nm else nm
            if not ("tmp" in nm or "fori" in nm or "snap" in nm):
                continue
            eng = nm.split("_", 1)[0]
            et = {"SP": mybir.EngineType.SP, "DVE": mybir.EngineType.DVE,
                  "Activation": mybir.EngineType.Activation,
                  "PE": mybir.EngineType.PE,
                  "Pool": mybir.EngineType.Pool}.get(eng)
            if et is None:
                continue
            try:
                self.nc.free_register(_br.RegisterHandle(nm, et))
                self._freed_regs.add(nm)
            except Exception:
                self._freed_regs.add(nm)

    def _mk_dram(self):
        g, nc = self.g, self.nc
        dm, S, T, DC, HO = g.dm, g.S, g.T, g.DC, g.HO
        D = {}
        D["x_own"] = nc.dram_tensor("x_own", [dm, T], F32, kind="ExternalInput")
        D["xs0"] = nc.dram_tensor("xs0", [dm, T], F32, kind="ExternalInput")
        D["wqkv"] = nc.dram_tensor("wqkv", [HO, 3, dm, 128], F32, kind="ExternalInput")
        D["wo"] = nc.dram_tensor("wo", [DC, HO * 128, 128], F32, kind="ExternalInput")
        D["wup"] = nc.dram_tensor("wup", [DC, dm, 128], F32, kind="ExternalInput")
        D["wgate"] = nc.dram_tensor("wgate", [dm, 4], F32, kind="ExternalInput")
        D["gbias"] = nc.dram_tensor("gbias", [4, 1], F32, kind="ExternalInput")
        D["wa"] = nc.dram_tensor("wa", [dm, 32], F32, kind="ExternalInput")
        D["wb"] = nc.dram_tensor("wb", [32, dm], F32, kind="ExternalInput")
        D["cosT"] = nc.dram_tensor("cosT", [128, S], F32, kind="ExternalInput")
        D["sinT"] = nc.dram_tensor("sinT", [128, S], F32, kind="ExternalInput")
        D["rank"] = nc.dram_tensor("rank", [1, 1], I32, kind="ExternalInput")

        D["x1"] = nc.dram_tensor("x1_d", [dm, T], F32)
        D["x1s0"] = nc.dram_tensor("x1s0_d", [dm, T], F32)
        D["x2"] = nc.dram_tensor("x2_d", [dm, T], F32)
        D["lnb"] = nc.dram_tensor("lnb_d", [dm, T], BF16)
        D["lnb0"] = nc.dram_tensor("lnb0_d", [dm, T], BF16)
        D["wgb"] = nc.dram_tensor("wgb_d", [dm, 4], BF16)
        D["wab"] = nc.dram_tensor("wab_d", [dm, 32], BF16)
        D["wbb"] = nc.dram_tensor("wbb_d", [32, dm], BF16)
        D["wupb"] = nc.dram_tensor("wupb_d", [DC, dm, 128], BF16)
        D["wqkvb"] = nc.dram_tensor("wqkvb_d", [HO, 3, dm, 128], BF16)
        D["rowb"] = nc.dram_tensor("rowb_d", [8 * 512], F32)
        D["qb"] = nc.dram_tensor("qb_d", [HO, 128, S], BF16)
        D["kb"] = nc.dram_tensor("kb_d", [HO, 128, S], BF16)
        D["vb"] = nc.dram_tensor("vb_d", [HO, 128, S], BF16)
        D["at"] = nc.dram_tensor("at_d", [HO, 128, S], BF16)
        D["qs"] = nc.dram_tensor("qs_d", [HO * (g.S // g.W) * 128], F32)
        D["kvis"] = nc.dram_tensor("kvis_d", [HO, 128, 512], BF16)
        D["opart"] = nc.dram_tensor("opart_d", [dm, S], BF16)

        aspc = "Local" if self.solo else "Shared"
        D["sh_lnx2"] = nc.dram_tensor("sh_lnx2", [dm, S], BF16, addr_space=aspc)
        D["sh_lnx20"] = nc.dram_tensor("sh_lnx20", [dm, S], BF16, addr_space=aspc)
        D["sh_imp"] = nc.dram_tensor("sh_imp", [2, 512], F32, addr_space=aspc)
        D["sh_op"] = nc.dram_tensor("sh_op", [2, dm, S], BF16, addr_space=aspc)
        D["sh_flag"] = nc.dram_tensor("sh_flag", [2, 1], I32, addr_space=aspc)

        D["y"] = nc.dram_tensor("y", [dm, T], F32, kind="ExternalOutput")
        self.D = D

    def _mk_const(self):
        g, nc = self.g, self.nc
        self.i_id = nc.inline_tensor(np.eye(128, dtype=BF_NP), name="i_id")
        self.i_onesr = nc.inline_tensor(np.ones((1, 128), np.float32), name="i_onesr")
        self.i_one = nc.inline_tensor(np.ones((1, 1), np.float32), name="i_one")
        self.i_onesc_bf = nc.inline_tensor(np.ones((128, 1), BF_NP), name="i_onesc_bf")
        self.i_onesc_f = nc.inline_tensor(np.ones((128, 1), np.float32),
                                          name="i_onesc_f")
        mlen = np.stack([np.arange(128, dtype=np.float32) + 128 * i + 1
                         for i in range(4)], axis=1)
        self.i_mlen = nc.inline_tensor(mlen, name="i_mlen")
        cm = np.zeros((128, 4 * 512), dtype=BF_NP)
        for i in range(4):
            svis = (np.arange(512)[None, :]
                    > (np.arange(128)[:, None] + 128 * i))
            cm[:, i * 512:(i + 1) * 512] = np.where(svis, -1e30, 0.0)
        self.i_cmask = nc.inline_tensor(cm, name="i_cmask")
        ssel = np.zeros((4, 32), dtype=BF_NP)
        for e in range(4):
            ssel[e, e * 8:(e + 1) * 8] = 1
        self.i_ssel = nc.inline_tensor(ssel, name="i_ssel")
        tri = np.zeros((128, 128), dtype=BF_NP)
        for v in range(128):
            tri[v, v + 1:] = 1
        self.i_tri_s = nc.inline_tensor(tri, name="i_tri_s")
        self.i_id_f = nc.inline_tensor(np.eye(128, dtype=np.float32),
                                       name="i_id_f")

    def _mk_persistent_sbuf(self):
        g = self.g
        dm, T, TOK, DC, W = g.dm, g.T, g.TOK, g.DC, g.W

        def sb(name, shape, dt):
            return self.es.enter_context(self.nc.sbuf_tensor(name, shape, dt))

        self.lnx = sb("lnx", [128, DC * W], BF16)
        self.zsel_sb = sb("zsel_sb", [32, T], BF16)
        self.gb_col = sb("gb_col", [4, 1], F32)
        self.sw_col = sb("sw_col", [4, 1], F32)
        self.sa_col = sb("sa_col", [32, 1], F32)
        self.c_id = sb("c_id", [128, 128], BF16)
        self.c_id_f = sb("c_id_f", [128, 128], F32)
        self.c_tri = sb("c_tri", [128, 128], BF16)
        self.c_ones_bf = sb("c_ones_bf", [128, 1], BF16)
        self.c_ones_f = sb("c_ones_f", [128, 1], F32)
        self.c_onesr_f = sb("c_onesr_f", [1, 128], F32)
        self.c_one_f = sb("c_one_f", [1, 1], F32)
        self.c_ssel = sb("c_ssel", [4, 32], BF16)
        self.c_mlen = sb("c_mlen", [128, 4], F32)
        self.c_cmask = sb("c_cmask", [128, 4 * 512], BF16)
        self.c_eps = sb("c_eps", [128, 1], F32)
        self.t_rank = sb("t_rank", [1, 1], I32)
        self.t_flag = sb("t_flag", [1, 1], I32)
        self.t_fval = sb("t_fval", [1, 1], I32)
        self.t_seed = sb("t_seed", [1, 1], F32)
        self.keep_bc = sb("keep_bc", [128, 512], BF16)
        self.cnt_cols = sb("cnt_cols", [128, 4], F32)
        self.cnt_tot = sb("cnt_tot", [128, 1], F32)

    @staticmethod
    def open_reg_names(rvs):
        out = set()
        for rv in rvs:
            try:
                for hd in rv.val.handles:
                    snap = hd.name
                    out.add(snap)
                    body = snap.split("_", 1)[1]
                    cnt = body.rsplit("_snap_", 1)[0]
                    out.add(cnt)
            except Exception:
                pass
        return out

    def shared_sems(self, kind, names):
        if not hasattr(self, "_shared"):
            self._shared = {}
        if kind not in self._shared:
            SEM, C = {}, {}
            for n in names:
                SEM[n] = self.es.enter_context(
                    self.nc.semaphore(f"{kind}_{n}"))
                C[n] = 0
                self.nsems += 1
            self._shared[kind] = (SEM, C)
        return self._shared[kind]

    def psems(self, names):
        SEM, C = {}, {}
        for n in names:
            SEM[n] = self.es.enter_context(
                self.nc.semaphore(f"{n}_p{self.phase_idx}"))
            C[n] = 0
            self.nsems += 1
        return SEM, C

    def want(self):
        self.phase_idx += 1
        return self.phase_idx <= self.n_phases

    def p_setup(self):
        if not self.want():
            return
        nc, D, g = self.nc, self.D, self.g
        DC = g.DC
        SEM, C = self.psems(["ld", "cast", "st", "pe", "row"])

        def bump(s_, n):
            C[s_] += n
            return C[s_]

        with ExitStack() as pes:
            wf = pes.enter_context(nc.sbuf_tensor("su_wf", [128, g.dm], F32))
            wb_ = pes.enter_context(nc.sbuf_tensor("su_wb", [128, g.dm], BF16))
            rowE = pes.enter_context(nc.sbuf_tensor("su_row", [1, 64], F32))
            ps_g = pes.enter_context(nc.psum_tensor("su_psg", [1, 64], F32))

            with nc.Block("setup") as blk:
                @blk.sync
                def _(sync):
                    for dst, src in [
                        (self.t_rank[:], D["rank"].ap()),
                        (self.c_id[:], self.i_id.ap()),
                        (self.c_id_f[:], self.i_id_f.ap()),
                        (self.c_tri[:], self.i_tri_s.ap()),
                        (self.c_ones_bf[:], self.i_onesc_bf.ap()),
                        (self.c_ones_f[:], self.i_onesc_f.ap()),
                        (self.c_onesr_f[:], self.i_onesr.ap()),
                        (self.c_one_f[:], self.i_one.ap()),
                        (self.c_ssel[:], self.i_ssel.ap()),
                        (self.c_mlen[:], self.i_mlen.ap()),
                        (self.c_cmask[:], self.i_cmask.ap()),
                        (self.gb_col[:], D["gbias"].ap()),
                    ]:
                        sync.dma_start(out=dst, in_=src).then_inc(SEM["ld"], 16)
                        sync.wait_ge(SEM["ld"], bump("ld", 16))

                @blk.vector
                def _(vec):
                    vec.memset(self.c_eps[:], EPS)

            with nc.Block("setup_rank") as blk:
                @blk.sync
                def _(sync):
                    sync.reg_load(self.rankreg["s"], self.t_rank[0:1, 0:1])

                @blk.vector
                def _(vec):
                    vec.reg_load(self.rankreg["v"], self.t_rank[0:1, 0:1])

                @blk.scalar
                def _(sc):
                    sc.reg_load(self.rankreg["a"], self.t_rank[0:1, 0:1])

                @blk.tensor
                def _(te):
                    te.reg_load(self.rankreg["t"], self.t_rank[0:1, 0:1])

                @blk.gpsimd
                def _(gp):
                    gp.reg_load(self.rankreg["g"], self.t_rank[0:1, 0:1])

            for nm, src, dst_d, width in [
                ("wg", D["wgate"], D["wgb"], 4),
                ("wa", D["wa"], D["wab"], 32),
            ]:
                with nc.Block(f"setup_{nm}") as blk:
                    @blk.sync
                    def _(sync, src=src, width=width):
                        sync.wait_ge(SEM["ld"], C["ld"])
                        sync.dma_start(
                            out=wf[:, 0:DC * width].rearrange(
                                "p (c e) -> p c e", c=DC),
                            in_=src.ap().rearrange("(c p) e -> p c e", p=128),
                        ).then_inc(SEM["ld"], 16)
                        sync.wait_ge(SEM["ld"], bump("ld", 16))

                    @blk.vector
                    def _(vec, width=width):
                        vec.wait_ge(SEM["ld"], C["ld"])
                        vec.tensor_copy(wb_[:, 0:DC * width],
                                        wf[:, 0:DC * width]
                                        ).then_inc(SEM["cast"], 1)
                        bump("cast", 1)

                    @blk.tensor
                    def _(te, width=width):
                        te.wait_ge(SEM["cast"], C["cast"])
                        for c in range(DC):
                            te.matmul(ps_g[0:1, 0:width], self.c_ones_bf[:],
                                      wb_[:, c * width:(c + 1) * width],
                                      start=(c == 0), stop=(c == DC - 1))
                        te.drain().then_inc(SEM["pe"], 1)
                        bump("pe", 1)

                    @blk.gpsimd
                    def _(gp, dst_d=dst_d, width=width):
                        gp.wait_ge(SEM["cast"], C["cast"])
                        gp.dma_start(
                            out=dst_d.ap().rearrange("(c p) e -> p c e", p=128),
                            in_=wb_[:, 0:DC * width].rearrange(
                                "p (c e) -> p c e", c=DC),
                        ).then_inc(SEM["st"], 16)
                        gp.wait_ge(SEM["st"], bump("st", 16))

                with nc.Block(f"setup_{nm}2") as blk:
                    @blk.vector
                    def _(vec, width=width):
                        vec.wait_ge(SEM["pe"], C["pe"])
                        vec.tensor_copy(rowE[:, 0:width], ps_g[0:1, 0:width]
                                        ).then_inc(SEM["row"], 1)
                        bump("row", 1)

                    @blk.gpsimd
                    def _(gp, nm=nm, width=width):
                        gp.wait_ge(SEM["row"], C["row"])
                        gp.dma_start(out=bass.AP(D["rowb"], 0, [[1, width]]),
                                     in_=rowE[0:1, 0:width]
                                     ).then_inc(SEM["st"], 16)
                        gp.wait_ge(SEM["st"], bump("st", 16))
                        dst = self.sw_col if nm == "wg" else self.sa_col
                        gp.dma_start(out=dst[:],
                                     in_=bass.AP(D["rowb"], 0,
                                                 [[1, width], [1, 1]])
                                     ).then_inc(SEM["st"], 16)
                        gp.wait_ge(SEM["st"], bump("st", 16))

            with nc.Block("setup_wbb") as blk:
                @blk.sync
                def _(sync):
                    sync.wait_ge(SEM["ld"], C["ld"])
                    sync.dma_start(out=wf[0:32, 0:g.dm],
                                   in_=D["wb"].ap()).then_inc(SEM["ld"], 16)
                    sync.wait_ge(SEM["ld"], bump("ld", 16))

                @blk.vector
                def _(vec):
                    vec.wait_ge(SEM["ld"], C["ld"])
                    vec.tensor_copy(wb_[0:32, 0:g.dm], wf[0:32, 0:g.dm]
                                    ).then_inc(SEM["cast"], 1)
                    bump("cast", 1)

                @blk.gpsimd
                def _(gp):
                    gp.wait_ge(SEM["cast"], C["cast"])
                    gp.dma_start(out=D["wbb"].ap(), in_=wb_[0:32, 0:g.dm]
                                 ).then_inc(SEM["st"], 16)
                    gp.wait_ge(SEM["st"], bump("st", 16))

    def ln_phase(self, tag, x_src, do_gate, lnb_dst=None, shared_dst=None,
                 preadd=False, x_save=None):
        if not self.want():
            return
        nc, D, g = self.nc, self.D, self.g
        dm, T, TOK, DC, S = g.dm, g.T, g.TOK, g.DC, g.S
        NTT = T // TOK
        lnb_dst = lnb_dst if lnb_dst is not None else D["lnb"]
        SEM, C = self.shared_sems(
            "ln", ["ld", "ld2", "ld3", "cast", "pe", "ev", "st", "row", "gx",
                   "aq", "lw", "ex", "sx"])

        def bump(s_, n):
            C[s_] += n
            return C[s_]

        ndma = 3 if preadd else 1
        x_eff = x_save if preadd else x_src
        NST = 2 if shared_dst is not None else 1
        NMM = 2 if do_gate else 0
        HDC = DC // 2

        with ExitStack() as pes:
            def sb(name, shape, dt):
                return pes.enter_context(nc.sbuf_tensor(name, shape, dt))

            def psum(name, shape):
                return pes.enter_context(nc.psum_tensor(name, shape, F32))

            xst = [sb(f"ln{tag}_x{i}", [128, TOK], F32) for i in range(2)]
            est = [sb(f"ln{tag}_e{i}", [128, TOK], F32) for i in range(2)]
            bst = [sb(f"ln{tag}_b{i}", [128, TOK], BF16) for i in range(2)]
            sqst = [sb(f"ln{tag}_q{i}", [128, TOK], BF16) for i in range(2)]
            obst = [sb(f"ln{tag}_o{i}", [128, TOK], BF16) for i in range(2)]
            s1acc = sb(f"ln{tag}_s1", [128, TOK], F32)
            s2acc = sb(f"ln{tag}_s2", [128, TOK], F32)
            rowA = sb(f"ln{tag}_rA", [1, TOK], F32)
            rowB = sb(f"ln{tag}_rB", [1, TOK], F32)
            rowC = sb(f"ln{tag}_rC", [1, TOK], F32)
            rowD = sb(f"ln{tag}_rD", [1, TOK], F32)
            if do_gate:
                wgst = sb(f"ln{tag}_wg", [128, 4], BF16)
                wast = sb(f"ln{tag}_wa", [128, 32], BF16)
                gacc = sb(f"ln{tag}_ga", [4, TOK], F32)
                zacc = sb(f"ln{tag}_za", [32, TOK], F32)
                rowE = sb(f"ln{tag}_rE", [1, 8 * TOK], F32)
                g4 = sb(f"ln{tag}_g4", [4, TOK], F32)
                g4b = sb(f"ln{tag}_g4b", [4, TOK], F32)
                z32 = sb(f"ln{tag}_z32", [32, TOK], F32)
                z32b = sb(f"ln{tag}_z32b", [32, TOK], F32)
                oh4 = sb(f"ln{tag}_oh4", [4, TOK], BF16)
                ps_g = [psum(f"ln{tag}_pg{i}", [4, TOK]) for i in range(2)]
                ps_z = [psum(f"ln{tag}_pz{i}", [32, TOK]) for i in range(2)]
            else:
                wgst = wast = gacc = zacc = rowE = g4 = g4b = None
                z32 = z32b = oh4 = ps_g = ps_z = None
            ps_s = psum(f"ln{tag}_ps", [1, TOK])
            ps_s2 = psum(f"ln{tag}_ps2", [1, TOK])
            ps_b1 = psum(f"ln{tag}_pb1", [128, TOK])
            ps_b2 = psum(f"ln{tag}_pb2", [128, TOK])

            env = dict(locals())
            for tt in range(NTT):
                self._ln_tt(tag, tt, env)
                self.free_temps()

        self.last_store = (SEM["st"], C["st"])
        self.free_temps()

    def _ln_tt(self, tag, tt, L):
        nc, D, g = self.nc, self.D, self.g
        dm, T, TOK, DC, S = g.dm, g.T, g.TOK, g.DC, g.S
        SEM, C, bump = L["SEM"], L["C"], L["bump"]
        xst, est, bst, sqst, obst = (L["xst"], L["est"], L["bst"], L["sqst"],
                                     L["obst"])
        s1acc, s2acc = L["s1acc"], L["s2acc"]
        rowA, rowB, rowC, rowD = L["rowA"], L["rowB"], L["rowC"], L["rowD"]
        ps_s, ps_s2, ps_b1, ps_b2 = L["ps_s"], L["ps_s2"], L["ps_b1"], L["ps_b2"]
        do_gate, preadd = L["do_gate"], L["preadd"]
        x_src, x_eff, x_save = L["x_src"], L["x_eff"], L["x_save"]
        shared_dst, lnb_dst = L["shared_dst"], L["lnb_dst"]
        ndma, NST, NMM, HDC = L["ndma"], L["NST"], L["NMM"], L["HDC"]
        wgst, wast, gacc, zacc = L["wgst"], L["wast"], L["gacc"], L["zacc"]
        ps_g, ps_z = L["ps_g"], L["ps_z"]

        col = tt * TOK

        ld0, cast0, pe0, ev0, aq0 = C["ld"], C["cast"], C["pe"], C["ev"], C["aq"]
        ld20 = C["ld2"]
        ld30 = C["ld3"]
        lw0 = C.get("lw", 0)
        ex0 = C.get("ex", 0)
        sx0 = C.get("sx", 0)

        def A_sync(sync, hc, par, first):
            if do_gate:
                if not first:
                    sync.wait_ge(SEM["pe"], hc * NMM + pe0)
                sync.wait_ge(SEM["lw"], hc * 32 + lw0)
                sync.dma_start(out=wgst[:],
                               in_=bass.AP(D["wgb"], hc * (128 * 4),
                                           [[4, 128], [1, 4]])
                               ).then_inc(SEM["lw"], 16)
                sync.wait_ge(SEM["lw"], hc * 32 + lw0 + 16)
                sync.dma_start(out=wast[:],
                               in_=bass.AP(D["wab"], hc * (128 * 32),
                                           [[32, 128], [1, 32]])
                               ).then_inc(SEM["lw"], 16)
            sync.wait_ge(SEM["ld"], hc * 16 + ld0)
            if not first:
                sync.wait_ge(SEM["ev"], hc + ev0 - 1)
                if preadd:
                    sync.wait_ge(SEM["sx"], hc * 16 + sx0 - 16)
            if preadd:
                sync.dma_start(out=est[par][:],
                               in_=bass.AP(x_src, hc * (128 * T) + col,
                                           [[T, 128], [1, TOK]])
                               ).then_inc(SEM["ld"], 16)
            else:
                sync.dma_start(out=xst[par][:],
                               in_=bass.AP(x_src, hc * (128 * T) + col,
                                           [[T, 128], [1, TOK]])
                               ).then_inc(SEM["ld"], 16)

        def A_ld2(sc, hc, par, first):
            sc.wait_ge(SEM["ld2"], hc * 16 + ld20)
            if not first:
                sc.wait_ge(SEM["ev"], hc + ev0 - 1)
            with sc.If_eq(self.rankreg["a"], 0):
                sc.dma_start(
                    out=obst[par][:],
                    in_=bass.AP(D["sh_op"], hc * (128 * S) + col,
                                [[S, 128], [1, TOK]]),
                ).then_inc(SEM["ld2"], 16)
            with sc.Else():
                sc.dma_start(
                    out=obst[par][:],
                    in_=bass.AP(D["sh_op"], hc * (128 * S) + T + col,
                                [[S, 128], [1, TOK]]),
                ).then_inc(SEM["ld2"], 16)

        def A_ld3(te, hc, par, first):
            te.wait_ge(SEM["ld3"], hc * 16 + ld30)
            if not first:
                te.wait_ge(SEM["ev"], hc + ev0 - 1)
            with te.If_eq(self.rankreg["g"], 0):
                te.dma_start(
                    out=sqst[par][:],
                    in_=bass.AP(D["sh_op"], hc * (128 * S) + dm * S + col,
                                [[S, 128], [1, TOK]]),
                ).then_inc(SEM["ld3"], 16)
            with te.Else():
                te.dma_start(
                    out=sqst[par][:],
                    in_=bass.AP(D["sh_op"],
                                hc * (128 * S) + dm * S + T + col,
                                [[S, 128], [1, TOK]]),
                ).then_inc(SEM["ld3"], 16)

        def A_vec(vec, hc, par, first):
            vec.wait_ge(SEM["ld"], hc * 16 + ld0 + 16)
            if preadd:
                vec.wait_ge(SEM["ld2"], hc * 16 + ld20 + 16)
                vec.wait_ge(SEM["ld3"], hc * 16 + ld30 + 16)
            if preadd:
                vec.tensor_add(xst[par][:], obst[par][:], sqst[par][:])
                vec.drain()
                vec.tensor_add(xst[par][:], xst[par][:], est[par][:])
                vec.drain().then_inc(SEM["ex"], 1)
            vec.tensor_copy(bst[par][:], xst[par][:]).then_inc(SEM["cast"], 1)
            vec.drain()
            vec.tensor_add(s1acc[:], s1acc[:], bst[par][:])
            vec.drain()
            vec.wait_ge(SEM["aq"], hc + aq0 + 1)
            vec.tensor_add(s2acc[:], s2acc[:], sqst[par][:])
            vec.drain()
            if do_gate:
                vec.wait_ge(SEM["pe"], hc * NMM + pe0 + NMM)
                vec.tensor_add(gacc[:], gacc[:], ps_g[par][:])
                vec.drain()
                vec.tensor_add(zacc[:], zacc[:], ps_z[par][:])
            vec.drain().then_inc(SEM["ev"], 1)

        def A_act(sc, hc, par, first):
            if preadd:
                A_ld2(sc, hc, par, first)
            sc.wait_ge(SEM["cast"], hc + cast0 + 1)
            if not first:
                sc.wait_ge(SEM["ev"], hc + ev0 - 1)
            sc.activation(sqst[par][:], bst[par][:], AF.Square
                          ).then_inc(SEM["aq"], 1)

        def A_pe(te, hc, par, first):
            te.wait_ge(SEM["cast"], hc + cast0 + 1)
            te.wait_ge(SEM["lw"], hc * 32 + lw0 + 32)
            if not first:
                te.wait_ge(SEM["ev"], hc + ev0 - 1)
            te.matmul(ps_g[par][:], wgst[:], bst[par][:], start=True, stop=True)
            te.matmul(ps_z[par][:], wast[:], bst[par][:], start=True, stop=True
                      ).then_inc(SEM["pe"], NMM)

        def hc_loop(eng, key, body, rankarg=False):
            body(eng, 0, 0, True)
            body(eng, 1, 1, False)
            if DC > 2:
                with eng.Fori(1, HDC) as it:
                    body(eng, it * 2, 0, False)
                    body(eng, it * 2 + 1, 1, False)

        with nc.Block(f"ln_{tag}a{tt}") as blk:
            @blk.sync
            def _(sync):
                hc_loop(sync, "s", A_sync)

            @blk.vector
            def _(vec):
                vec.memset(s1acc[:], 0.0)
                vec.memset(s2acc[:], 0.0)
                if do_gate:
                    vec.memset(gacc[:], 0.0)
                    vec.memset(zacc[:], 0.0)
                vec.drain()
                hc_loop(vec, "v", A_vec)

            @blk.scalar
            def _(sc):
                hc_loop(sc, "a", A_act)

            if do_gate:
                @blk.tensor
                def _(te):
                    hc_loop(te, "t", A_pe)

            if preadd:
                @blk.gpsimd
                def _(gp):
                    def G_save(gpe, hc, par, first):
                        A_ld3(gpe, hc, par, first)
                        gpe.wait_ge(SEM["ex"], hc + ex0 + 1)
                        gpe.wait_ge(SEM["sx"], hc * 16 + sx0)
                        gpe.dma_start(
                            out=bass.AP(x_save, hc * (128 * T) + col,
                                        [[T, 128], [1, TOK]]),
                            in_=xst[par][:],
                        ).then_inc(SEM["sx"], 16)
                    hc_loop(gp, "g", G_save)
                    gp.wait_ge(SEM["sx"], sx0 + DC * 16)

        bump("ld", DC * 16)
        if do_gate:
            bump("lw", DC * 32)
        bump("cast", DC)
        bump("aq", DC)
        bump("pe", DC * NMM)
        bump("ev", DC)
        if preadd:
            bump("ld2", DC * 16)
            bump("ld3", DC * 16)
            bump("ex", DC)
            bump("sx", DC * 16)
        self.free_temps()

        pe1, row1, gx1 = C["pe"], C["row"], C["gx"]
        with nc.Block(f"ln_{tag}b{tt}") as blk:
            @blk.tensor
            def _(te):
                te.matmul(ps_s[0:1, 0:TOK], self.c_ones_f[:], s1acc[:],
                          start=True, stop=True)
                te.matmul(ps_s2[0:1, 0:TOK], self.c_ones_f[:], s2acc[:],
                          start=True, stop=True)
                te.drain().then_inc(SEM["pe"], 1)
                te.wait_ge(SEM["row"], row1 + 2)
                te.matmul(ps_b1[:, 0:TOK], self.c_onesr_f[:], rowB[:],
                          start=True, stop=True)
                te.matmul(ps_b2[:, 0:TOK], self.c_onesr_f[:], rowD[:],
                          start=True, stop=True)
                te.drain().then_inc(SEM["pe"], 1)

            @blk.vector
            def _(vec):
                vec.wait_ge(SEM["pe"], pe1 + 1)
                vec.tensor_scalar_mul(rowA[:], ps_s[0:1, 0:TOK], 1.0 / dm)
                vec.drain()
                vec.tensor_scalar_mul(rowC[:], ps_s2[0:1, 0:TOK], 1.0 / dm)
                vec.drain()
                vec.tensor_mul(rowB[:], rowA[:], rowA[:])
                vec.drain()
                vec.tensor_sub(rowC[:], rowC[:], rowB[:])
                vec.drain().then_inc(SEM["row"], 1)
                vec.wait_ge(SEM["gx"], gx1 + 2)
                vec.tensor_mul(rowD[:], rowA[:], rowB[:])
                vec.drain()
                vec.tensor_scalar_mul(rowD[:], rowD[:], -1.0)
                vec.drain().then_inc(SEM["row"], 1)

            @blk.scalar
            def _(sc):
                sc.wait_ge(SEM["row"], row1 + 1)
                sc.activation(rowC[:], rowC[:], AF.Ln,
                              bias=self.c_eps[0:1, 0:1]).then_inc(SEM["gx"], 1)
                sc.drain()
                sc.activation(rowB[:], rowC[:], AF.Exp, scale=-0.5
                              ).then_inc(SEM["gx"], 1)

        bump("pe", 2)
        bump("row", 2)
        bump("gx", 2)

        ld2, ev2, st2 = C["ld2"], C["ev"], C["st"]

        def C_act(sc, hc, par, first):
            sc.wait_ge(SEM["ld2"], hc * 16 + ld2)
            if not first:
                sc.wait_ge(SEM["ev"], hc + ev2 - 1)
            sc.dma_start(out=xst[par][:],
                         in_=bass.AP(x_eff, hc * (128 * T) + col,
                                     [[T, 128], [1, TOK]])
                         ).then_inc(SEM["ld2"], 16)

        def C_vec(vec, hc, par, first):
            vec.wait_ge(SEM["ld2"], hc * 16 + ld2 + 16)
            if not first:
                vec.wait_ge(SEM["st"], hc * (NST * 16) + st2 - NST * 16)
            vec.tensor_mul(est[par][:], xst[par][:], ps_b1[:, 0:TOK])
            vec.drain()
            vec.tensor_tensor(bst[par][:], est[par][:], ps_b2[:, 0:TOK], ALU.add)
            vec.drain().then_inc(SEM["ev"], 1)

        def C_gp(gp, hc, par, first):
            gp.wait_ge(SEM["ev"], hc + ev2 + 1)
            gp.wait_ge(SEM["st"], hc * (NST * 16) + st2)
            gp.dma_start(out=bass.AP(lnb_dst, hc * (128 * T) + col,
                                     [[T, 128], [1, TOK]]),
                         in_=bst[par][:]).then_inc(SEM["st"], 16)
            if shared_dst is not None:
                gp.wait_ge(SEM["st"], hc * (NST * 16) + st2 + 16)
                S_ = self.g.S
                with gp.If_eq(self.rankreg["g"], 0):
                    gp.dma_start(
                        out=bass.AP(shared_dst, hc * (128 * S_) + col,
                                    [[S_, 128], [1, TOK]]),
                        in_=bst[par][:]).then_inc(SEM["st"], 16)
                with gp.Else():
                    gp.dma_start(
                        out=bass.AP(shared_dst, T + hc * (128 * S_) + col,
                                    [[S_, 128], [1, TOK]]),
                        in_=bst[par][:]).then_inc(SEM["st"], 16)

        with nc.Block(f"ln_{tag}c{tt}") as blk:
            @blk.scalar
            def _(sc):
                hc_loop(sc, "a", C_act)

            @blk.vector
            def _(vec):
                hc_loop(vec, "v", C_vec)

            @blk.gpsimd
            def _(gp):
                hc_loop(gp, "g", C_gp)
                gp.wait_ge(SEM["st"], st2 + DC * NST * 16)

        bump("ld2", DC * 16)
        bump("ev", DC)
        bump("st", DC * NST * 16)
        self.free_temps()

        if do_gate:
            rowE, g4, g4b = L["rowE"], L["g4"], L["g4b"]
            z32, z32b, oh4 = L["z32"], L["z32b"], L["oh4"]
            st3, row3, cast3, pe3, ev3 = (C["st"], C["row"], C["cast"],
                                          C["pe"], C["ev"])
            with nc.Block(f"ln_{tag}d{tt}") as blk:
                @blk.vector
                def _(vec):
                    vec.tensor_mul(g4[:], gacc[:], ps_b1[0:4, 0:TOK])
                    vec.drain()
                    vec.tensor_single_scalar(g4b[:], ps_b2[0:4, 0:TOK],
                                             self.sw_col[:], ALU.mult)
                    vec.drain()
                    vec.tensor_add(g4[:], g4[:], g4b[:])
                    vec.drain()
                    vec.tensor_single_scalar(g4[:], g4[:], self.gb_col[:],
                                             ALU.add)
                    vec.drain()
                    vec.tensor_mul(z32[:], zacc[:], ps_b1[0:32, 0:TOK])
                    vec.drain()
                    vec.tensor_single_scalar(z32b[:], ps_b2[0:32, 0:TOK],
                                             self.sa_col[:], ALU.mult)
                    vec.drain()
                    vec.tensor_add(z32[:], z32[:], z32b[:])
                    vec.drain().then_inc(SEM["row"], 1)
                    vec.wait_ge(SEM["st"], st3 + 32)
                    le = [rowE[:, e * TOK:(e + 1) * TOK] for e in range(4)]
                    oh = [rowE[:, (4 + e) * TOK:(5 + e) * TOK] for e in range(4)]
                    vec.tensor_max(rowA[:], le[0], le[1])
                    vec.drain()
                    vec.tensor_max(rowB[:], le[2], le[3])
                    vec.drain()
                    vec.tensor_max(rowA[:], rowA[:], rowB[:])
                    vec.drain()
                    for e in range(4):
                        vec.tensor_tensor(oh[e], le[e], rowA[:], ALU.is_ge)
                        vec.drain()
                    vec.tensor_copy(rowB[:], oh[0])
                    vec.drain()
                    for e in range(1, 4):
                        vec.tensor_mul(rowC[:], oh[e], rowB[:])
                        vec.drain()
                        vec.tensor_sub(oh[e], oh[e], rowC[:])
                        vec.drain()
                        if e < 3:
                            vec.tensor_add(rowB[:], rowB[:], oh[e])
                            vec.drain()
                    vec.drain().then_inc(SEM["row"], 1)
                    vec.wait_ge(SEM["st"], st3 + 64)
                    vec.tensor_copy(oh4[:], g4b[:])
                    vec.drain().then_inc(SEM["cast"], 1)
                    vec.wait_ge(SEM["pe"], pe3 + 1)
                    vec.tensor_mul(z32[:], z32[:], ps_z[0][:])
                    vec.drain()
                    vec.tensor_copy(self.zsel_sb[:, col:col + TOK], z32[:])
                    vec.drain().then_inc(SEM["ev"], 1)

                @blk.gpsimd
                def _(gp):
                    gp.wait_ge(SEM["row"], row3 + 1)
                    gp.wait_ge(SEM["st"], st3)
                    gp.dma_start(out=bass.AP(D["rowb"], 0, [[TOK, 4], [1, TOK]]),
                                 in_=g4[:]).then_inc(SEM["st"], 16)
                    gp.wait_ge(SEM["st"], st3 + 16)
                    gp.dma_start(out=rowE[:, 0:4 * TOK],
                                 in_=bass.AP(D["rowb"], 0, [[1, 4 * TOK]])
                                 ).then_inc(SEM["st"], 16)
                    gp.wait_ge(SEM["row"], row3 + 2)
                    gp.wait_ge(SEM["st"], st3 + 32)
                    gp.dma_start(out=bass.AP(D["rowb"], 0, [[1, 4 * TOK]]),
                                 in_=rowE[:, 4 * TOK:8 * TOK]
                                 ).then_inc(SEM["st"], 16)
                    gp.wait_ge(SEM["st"], st3 + 48)
                    gp.dma_start(out=g4b[:],
                                 in_=bass.AP(D["rowb"], 0, [[TOK, 4], [1, TOK]])
                                 ).then_inc(SEM["st"], 16)
                    gp.wait_ge(SEM["st"], st3 + 64)

                @blk.tensor
                def _(te):
                    te.wait_ge(SEM["cast"], cast3 + 1)
                    te.matmul(ps_z[0][:], self.c_ssel[:], oh4[:],
                              start=True, stop=True)
                    te.drain().then_inc(SEM["pe"], 1)

            bump("row", 2)
            bump("st", 64)
            bump("cast", 1)
            bump("pe", 1)
            bump("ev", 1)

    def _window_load(self, blkname, SEM, C, bump, src, src_T, wofs):
        nc = self.nc
        DC, W = self.g.DC, self.g.W
        with nc.Block(blkname) as blk:
            @blk.sync
            def _(sync):
                for c in range(DC):
                    sync.wait_ge(SEM["ld"], C["ld"] + c * 16)
                    sync.dma_start(
                        out=self.lnx[:, c * W:(c + 1) * W],
                        in_=bass.AP(src, c * (128 * src_T) + wofs,
                                    [[src_T, 128], [1, W]]),
                    ).then_inc(SEM["ld"], 16)
                sync.wait_ge(SEM["ld"], C["ld"] + DC * 16)
        bump("ld", DC * 16)

    def ffn_phase(self, tag, w_f32_src, w_bf_src, cache_dst, x_src, out_dst,
                  lnb_src=None):
        if not self.want():
            return
        nc, D, g = self.nc, self.D, self.g
        dm, T, TOK, DC, W, NTW, NHP = (g.dm, g.T, g.TOK, g.DC, g.W, g.NTW,
                                       g.NHP)
        lnb_src = lnb_src if lnb_src is not None else D["lnb"]
        do_cast = w_f32_src is not None
        SEM, C = self.shared_sems("ffn", ["ld", "lw", "pe", "ev", "st",
                                          "cast"])

        def bump(s_, n):
            C[s_] += n
            return C[s_]

        with ExitStack() as pes:
            def sb(name, shape, dt):
                return pes.enter_context(nc.sbuf_tensor(name, shape, dt))

            wsf = sb(f"ffn{tag}_wf", [128, DC * 128], F32)
            wsb = [sb(f"ffn{tag}_wb{i}", [128, DC * 128], BF16)
                   for i in range(2)]
            wbst = [sb(f"ffn{tag}_lb{i}", [32, 128], BF16) for i in range(2)]
            xst = [sb(f"ffn{tag}_x{i}", [128, TOK], F32) for i in range(2)]
            est = [sb(f"ffn{tag}_e{i}", [128, TOK], F32) for i in range(2)]
            ps_m = [pes.enter_context(nc.psum_tensor(f"ffn{tag}_pm{i}",
                                                     [128, TOK], F32))
                    for i in range(4)]
            env = dict(locals())
            for hp in range(NHP):
                self._window_load(f"ffn_{tag}w{hp}", SEM, C, bump,
                                  lnb_src, T, hp * W)
                self._ffn_window(tag, hp, env)
                self.free_temps()
        self.last_store = (SEM["st"], C["st"])
        self.free_temps()

    def _ffn_window(self, tag, hp, L):
        nc, D, g = self.nc, self.D, self.g
        dm, T, TOK, DC, W, NTW = g.dm, g.T, g.TOK, g.DC, g.W, g.NTW
        SEM, C, bump = L["SEM"], L["C"], L["bump"]
        wsf, wsb, wbst = L["wsf"], L["wsb"], L["wbst"]
        xst, est, ps_m = L["xst"], L["est"], L["ps_m"]
        do_cast = L["do_cast"]
        w_f32_src, w_bf_src = L["w_f32_src"], L["w_bf_src"]
        cache_dst, x_src, out_dst = L["cache_dst"], L["x_src"], L["out_dst"]
        SELT = dm * 128
        cache = cache_dst is not None and hp == 0
        PER = NTW + (1 if cache else 0)
        COFS = 1 if cache else 0
        wofs = hp * W

        lw0, ld0, pe0, ev0, st0 = (C[k] for k in ["lw", "ld", "pe", "ev", "st"])
        cast0 = C.get("cast", 0)

        def F_sync(sync, oc, par, first):
            sync.wait_ge(SEM["lw"], oc * 32 + lw0)
            if do_cast:
                if not first:
                    sync.wait_ge(SEM["cast"], oc + cast0)
                sync.dma_start(
                    out=wsf[:].rearrange("p (c o) -> p c o", c=DC),
                    in_=bass.AP(w_f32_src, oc * SELT,
                                [[128, 128], [128 * 128, DC], [1, 128]]),
                ).then_inc(SEM["lw"], 16)
            else:
                if not first:
                    sync.wait_ge(SEM["pe"], oc * NTW + pe0 - NTW)
                sync.dma_start(
                    out=wsb[par][:].rearrange("p (c o) -> p c o", c=DC),
                    in_=bass.AP(w_bf_src, oc * SELT,
                                [[128, 128], [128 * 128, DC], [1, 128]]),
                ).then_inc(SEM["lw"], 16)
            sync.wait_ge(SEM["lw"], oc * 32 + lw0 + 16)
            if not first:
                sync.wait_ge(SEM["pe"], oc * NTW + pe0 - NTW)
            sync.dma_start(out=wbst[par][:],
                           in_=bass.AP(D["wbb"], oc * 128,
                                       [[dm, 32], [1, 128]])
                           ).then_inc(SEM["lw"], 16)
            for tt in range(NTW):
                kp = tt % 2
                sync.wait_ge(SEM["ld"], oc * (NTW * 16) + ld0 + tt * 16)
                if not (first and tt < 2):
                    sync.wait_ge(SEM["ev"], oc * NTW + ev0 + tt - 1)
                sync.dma_start(out=xst[kp][:],
                               in_=bass.AP(x_src, oc * (128 * T) + wofs + tt * TOK,
                                           [[T, 128], [1, TOK]])
                               ).then_inc(SEM["ld"], 16)

        def F_vec(vec, oc, par, first):
            if do_cast:
                vec.wait_ge(SEM["lw"], oc * 32 + lw0 + 16)
                if not first:
                    vec.wait_ge(SEM["pe"], oc * NTW + pe0 - NTW)
                vec.tensor_copy(wsb[par][:], wsf[:]).then_inc(SEM["cast"], 1)
                vec.drain()
            for tt in range(NTW):
                kp = tt % 2
                vec.wait_ge(SEM["pe"], oc * NTW + pe0 + tt + 1)
                vec.wait_ge(SEM["ld"], oc * (NTW * 16) + ld0 + (tt + 1) * 16)
                if not (first and tt < 2):
                    if tt >= 1:
                        vec.wait_ge(SEM["st"], oc * (PER * 16)
                                    + st0 + (COFS + tt - 1) * 16)
                    else:
                        vec.wait_ge(SEM["st"], oc * (PER * 16)
                                    + st0 + (COFS + NTW - 1 - PER) * 16)
                vec.scalar_tensor_tensor(est[kp][:],
                                         ps_m[2 * par + tt % 2][:], 0.5,
                                         xst[kp][:], ALU.mult, ALU.add
                                         ).then_inc(SEM["ev"], 1)
                vec.drain()

        def F_pe(te, oc, par, first):
            if do_cast:
                te.wait_ge(SEM["cast"], oc + cast0 + 1)
            te.wait_ge(SEM["lw"], oc * 32 + lw0 + 32)
            if not first:
                te.wait_ge(SEM["ev"], oc * NTW + ev0 - NTW)
            for tt in range(NTW):
                bank = ps_m[2 * par + tt % 2][:]
                for hc in range(DC):
                    te.matmul(bank, wsb[par][:, hc * 128:(hc + 1) * 128],
                              self.lnx[:, hc * W + tt * TOK:
                                       hc * W + tt * TOK + TOK],
                              start=(hc == 0), stop=False)
                te.matmul(bank, wbst[par][:],
                          self.zsel_sb[:, wofs + tt * TOK:
                                       wofs + (tt + 1) * TOK],
                          start=False, stop=True).then_inc(SEM["pe"], 1)

        def F_gp(gp, oc, par, first):
            if cache:
                gp.wait_ge(SEM["cast"], oc + cast0 + 1)
                gp.wait_ge(SEM["st"], oc * (PER * 16) + st0)
                gp.dma_start(
                    out=bass.AP(cache_dst, oc * SELT,
                                [[128, 128], [128 * 128, DC], [1, 128]]),
                    in_=wsb[par][:].rearrange("p (c o) -> p c o", c=DC),
                ).then_inc(SEM["st"], 16)
            for tt in range(NTW):
                kp = tt % 2
                gp.wait_ge(SEM["ev"], oc * NTW + ev0 + tt + 1)
                gp.wait_ge(SEM["st"], oc * (PER * 16) + st0 + (COFS + tt) * 16)
                gp.dma_start(out=bass.AP(out_dst, oc * (128 * T) + wofs + tt * TOK,
                                         [[T, 128], [1, TOK]]),
                             in_=est[kp][:]).then_inc(SEM["st"], 16)

        def stream(eng, body, key):
            body(eng, 0, 0, True)
            body(eng, 1, 1, False)
            if DC > 2:
                with eng.Fori(1, DC // 2) as it:
                    body(eng, it * 2, 0, False)
                    body(eng, it * 2 + 1, 1, False)

        with nc.Block(f"ffn_{tag}m{hp}") as blk:
            @blk.sync
            def _(sync):
                stream(sync, F_sync, "s")

            @blk.vector
            def _(vec):
                stream(vec, F_vec, "v")

            @blk.tensor
            def _(te):
                stream(te, F_pe, "t")

            @blk.gpsimd
            def _(gp):
                stream(gp, F_gp, "g")
                gp.wait_ge(SEM["st"], st0 + DC * PER * 16)

        bump("lw", DC * 32)
        bump("ld", DC * NTW * 16)
        if do_cast:
            bump("cast", DC)
        bump("pe", DC * NTW)
        bump("ev", DC * NTW)
        bump("st", DC * PER * 16)

    def flag_write(self, fid):
        if not self.want():
            return
        if self.solo:
            return
        nc, D = self.nc, self.D
        st_sem, st_cnt = self.last_store if self.last_store else (None, 0)
        SEM, _ = self.psems(["fw"])
        with nc.Block(f"flw{fid}") as blk:
            @blk.gpsimd
            def _(gp):
                gp.memset(self.t_fval[:], fid)
                gp.drain()
                if st_sem is not None:
                    gp.wait_ge(st_sem, st_cnt)
                with gp.If_eq(self.rankreg["g"], 0):
                    gp.dma_start(out=bass.AP(D["sh_flag"], 0, [[1, 1], [1, 1]]),
                                 in_=self.t_fval[:]).then_inc(SEM["fw"], 16)
                with gp.Else():
                    gp.dma_start(out=bass.AP(D["sh_flag"], 1, [[1, 1], [1, 1]]),
                                 in_=self.t_fval[:]).then_inc(SEM["fw"], 16)
                gp.wait_ge(SEM["fw"], 16)

    def flag_poll(self, fid):
        if not self.want():
            return
        if self.solo:
            return
        nc, D = self.nc, self.D
        SEM, _ = self.psems(["pl"])
        with nc.Block(f"flp{fid}") as blk:
            @blk.gpsimd
            def _(gp):
                with gp.register(f"rpf{fid}") as rf, \
                     gp.register(f"rpc{fid}") as rc:
                    gp.reg_mov(rf, 0)
                    gp.reg_mov(rc, 0)
                    with gp.While(lambda: gp.compute_val(rf) < fid):
                        with gp.If_eq(self.rankreg["g"], 0):
                            gp.dma_start(out=self.t_flag[:],
                                         in_=bass.AP(D["sh_flag"], 1,
                                                     [[1, 1], [1, 1]])
                                         ).then_inc(SEM["pl"], 16)
                        with gp.Else():
                            gp.dma_start(out=self.t_flag[:],
                                         in_=bass.AP(D["sh_flag"], 0,
                                                     [[1, 1], [1, 1]])
                                         ).then_inc(SEM["pl"], 16)
                        gp.reg_add(rc, rc, 16)
                        gp.wait_ge(SEM["pl"], rc)
                        gp.reg_load(rf, self.t_flag[0:1, 0:1])

    def seed_sems(self, SEM, C, dma=None, eng=None):
        nc, D = self.nc, self.D
        with nc.Block(f"seed{self.phase_idx}") as blk:
            @blk.gpsimd
            def _(gp):
                for nm, cnt in (dma or {}).items():
                    for k in range(cnt // 16):
                        gp.wait_ge(SEM[nm], C[nm] + k * 16)
                        gp.dma_start(out=self.t_seed[:],
                                     in_=bass.AP(D["rowb"], 0, [[1, 1], [1, 1]])
                                     ).then_inc(SEM[nm], 16)
                for nm, n in (eng or {}).items():
                    gp.drain().then_inc(SEM[nm], n)
        for nm, cnt in (dma or {}).items():
            C[nm] += cnt
        for nm, n in (eng or {}).items():
            C[nm] += n

    def qkv_phase(self):
        if not self.want():
            return
        nc, D, g = self.nc, self.D, self.g
        dm, S, T, TOK, DC, W, NTW, HO = (g.dm, g.S, g.T, g.TOK, g.DC, g.W,
                                         g.NTW, g.HO)
        NBLK = g.NBLK
        SELT = dm * 128
        SEM, C = self.psems(["ld", "lw", "cast", "pe", "ev", "st"])
        self.seed_sems(SEM, C, dma={"st": 96}, eng={"pe": 6, "ev": 6})
        L0, S0, P0, E0, LD0 = C["lw"], C["st"], C["pe"], C["ev"], C["ld"]
        PEB, STB = HO * 3 * NTW, HO * 3 * NTW * 16
        LWB0, LWBd = HO * 96, HO * 48
        LDB = 48

        with ExitStack() as pes:
            def sb(name, shape, dt):
                return pes.enter_context(nc.sbuf_tensor(name, shape, dt))

            wsf = sb("qk_wf", [128, DC * 128], F32)
            wsb = [sb(f"qk_wb{i}", [128, DC * 128], BF16) for i in range(3)]
            c_cos = sb("qk_cos", [128, W], F32)
            c_sin = sb("qk_sin", [128, W], F32)
            ro1 = sb("qk_ro1", [128, TOK], F32)
            rt = sb("qk_rt", [128, TOK], F32)
            r1 = sb("qk_r1", [128, TOK], F32)
            obf = [[sb(f"qk_o{w}_{i}", [128, TOK], BF16) for i in range(2)]
                   for w in range(3)]
            ps_m = [pes.enter_context(nc.psum_tensor(f"qk_pm{i}",
                                                     [128, TOK], F32))
                    for i in range(6)]

            def win_sync(sync, b, bofs_py):
                ld_s = LD0 + b * LDB
                if not (isinstance(b, int) and b == 0):
                    sync.wait_ge(SEM["pe"], P0 + PEB * b)
                    sync.wait_ge(SEM["ev"], E0 + PEB * b)
                sync.wait_ge(SEM["ld"], ld_s)
                sync.dma_start(
                    out=self.lnx[:].rearrange("p (c w) -> p c w", c=DC),
                    in_=bass.AP(D["sh_lnx2"], b * W,
                                [[S, 128], [128 * S, DC], [1, W]])
                    ).then_inc(SEM["ld"], 16)
                sync.wait_ge(SEM["ld"], ld_s + 16)
                sync.dma_start(out=c_cos[:],
                               in_=bass.AP(D["cosT"], b * W, [[S, 128], [1, W]])
                               ).then_inc(SEM["ld"], 16)
                sync.wait_ge(SEM["ld"], ld_s + 32)
                sync.dma_start(out=c_sin[:],
                               in_=bass.AP(D["sinT"], b * W, [[S, 128], [1, W]])
                               ).then_inc(SEM["ld"], 16)
                sync.wait_ge(SEM["ld"], ld_s + 48)

            def bases(b, fc, w, cast_blk):
                B = {}
                B["pe"] = P0 + PEB * b + fc * (3 * NTW) + w * NTW
                B["ev"] = E0 + PEB * b + fc * (3 * NTW) + w * NTW
                B["st"] = S0 + STB * b + fc * 96 + w * 32
                if cast_blk:
                    B["lw"] = L0 + fc * 96 + w * 32
                else:
                    B["lw"] = L0 + LWB0 + (b - 1) * LWBd + fc * 48 + w * 16
                B["cast"] = fc * 3 + w
                B["ld"] = LD0 + (b + 1) * LDB
                return B

            def P_sync(sync, b, fc, w, cast_blk, dst_d):
                B = bases(b, fc, w, cast_blk)
                if cast_blk:
                    sync.wait_ge(SEM["cast"], B["cast"])
                    sync.wait_ge(SEM["lw"], B["lw"])
                    sync.dma_start(
                        out=wsf[:].rearrange("p (c o) -> p c o", c=DC),
                        in_=bass.AP(D["wqkv"], (fc * 3 + w) * SELT,
                                    [[128, 128], [128 * 128, DC], [1, 128]]),
                    ).then_inc(SEM["lw"], 16)
                    sync.wait_ge(SEM["lw"], B["lw"] + 16)
                    sync.wait_ge(SEM["cast"], B["cast"] + 1)
                    sync.dma_start(
                        out=bass.AP(D["wqkvb"], (fc * 3 + w) * SELT,
                                    [[128, 128], [128 * 128, DC], [1, 128]]),
                        in_=wsb[w][:].rearrange("p (c o) -> p c o", c=DC),
                    ).then_inc(SEM["lw"], 16)
                else:
                    sync.wait_ge(SEM["pe"], B["pe"] - 3 * NTW + NTW)
                    sync.wait_ge(SEM["lw"], B["lw"])
                    sync.dma_start(
                        out=wsb[w][:].rearrange("p (c o) -> p c o", c=DC),
                        in_=bass.AP(D["wqkvb"], (fc * 3 + w) * SELT,
                                    [[128, 128], [128 * 128, DC], [1, 128]]),
                    ).then_inc(SEM["lw"], 16)

            def P_vec(vec, b, fc, w, cast_blk, dst_d):
                B = bases(b, fc, w, cast_blk)
                rope = (w < 2)
                if cast_blk:
                    vec.wait_ge(SEM["lw"], B["lw"] + 16)
                    vec.wait_ge(SEM["pe"], B["pe"] - 3 * NTW + NTW)
                    vec.tensor_copy(wsb[w][:], wsf[:]).then_inc(SEM["cast"], 1)
                    vec.drain()
                for nt in range(NTW):
                    kp = nt % 2
                    bank = ps_m[2 * w + kp][:]
                    vec.wait_ge(SEM["pe"], B["pe"] + nt + 1)
                    vec.wait_ge(SEM["st"], B["st"] - 96 + (nt + 1) * 16)
                    if rope:
                        vec.tensor_mul(ro1[:], bank,
                                       c_cos[:, nt * TOK:(nt + 1) * TOK])
                        vec.tensor_scalar_mul(rt[0:64, :],
                                              ps_m[2 * w + kp][64:128, :], -1.0)
                        vec.tensor_copy(rt[64:128, :],
                                        ps_m[2 * w + kp][0:64, :])
                        vec.drain()
                        vec.tensor_mul(r1[:], rt[:],
                                       c_sin[:, nt * TOK:(nt + 1) * TOK])
                        vec.drain()
                        vec.tensor_add(obf[w][kp][:], ro1[:], r1[:])
                    else:
                        vec.tensor_copy(obf[w][kp][:], bank)
                    vec.drain().then_inc(SEM["ev"], 1)

            def P_pe(te, b, fc, w, cast_blk, dst_d):
                B = bases(b, fc, w, cast_blk)
                if cast_blk:
                    te.wait_ge(SEM["cast"], B["cast"] + 1)
                else:
                    te.wait_ge(SEM["lw"], B["lw"] + 16)
                te.wait_ge(SEM["ev"], B["ev"] - 3 * NTW + NTW)
                te.wait_ge(SEM["ld"], B["ld"])
                for nt in range(NTW):
                    bank = ps_m[2 * w + nt % 2][:]
                    for hc in range(DC):
                        mm = te.matmul(bank, wsb[w][:, hc * 128:(hc + 1) * 128],
                                       self.lnx[:, hc * W + nt * TOK:
                                                hc * W + nt * TOK + TOK],
                                       start=(hc == 0), stop=(hc == DC - 1))
                    mm.then_inc(SEM["pe"], 1)

            def P_gp(gp, b, fc, w, cast_blk, dst_d):
                B = bases(b, fc, w, cast_blk)
                for nt in range(NTW):
                    kp = nt % 2
                    gp.wait_ge(SEM["ev"], B["ev"] + nt + 1)
                    gp.wait_ge(SEM["st"], B["st"] + nt * 16)
                    gp.dma_start(
                        out=bass.AP(dst_d, fc * (128 * S) + b * W + nt * TOK,
                                    [[S, 128], [1, TOK]]),
                        in_=obf[w][kp][:]).then_inc(SEM["st"], 16)

            DSTS = [D["qb"], D["kb"], D["vb"]]

            def strips(eng, body, b, cast_blk):
                with eng.Fori(0, HO) as fc:
                    for w in range(3):
                        body(eng, b, fc, w, cast_blk, DSTS[w])

            with nc.Block("qkv0") as blk:
                @blk.sync
                def _(sync):
                    win_sync(sync, 0, 0)
                    strips(sync, P_sync, 0, True)

                @blk.vector
                def _(vec):
                    strips(vec, P_vec, 0, True)

                @blk.tensor
                def _(te):
                    strips(te, P_pe, 0, True)

                @blk.gpsimd
                def _(gp):
                    strips(gp, P_gp, 0, True)
            self.free_temps()
            if NBLK > 1:
                with nc.Block("qkvd") as blk:
                    @blk.sync
                    def _(sync):
                        with sync.Fori(1, NBLK) as b:
                            win_sync(sync, b, None)
                            strips(sync, P_sync, b, False)

                    @blk.vector
                    def _(vec):
                        with vec.Fori(1, NBLK) as b:
                            strips(vec, P_vec, b, False)

                    @blk.tensor
                    def _(te):
                        with te.Fori(1, NBLK) as b:
                            strips(te, P_pe, b, False)

                    @blk.gpsimd
                    def _(gp):
                        with gp.Fori(1, NBLK) as b:
                            strips(gp, P_gp, b, False)
            with nc.Block("qkve") as blk:
                @blk.gpsimd
                def _(gp):
                    gp.wait_ge(SEM["st"], S0 + STB * NBLK)

        C["ld"] += NBLK * LDB
        C["lw"] += LWB0 + (NBLK - 1) * LWBd
        C["cast"] += HO * 3
        C["pe"] += PEB * NBLK
        C["ev"] += PEB * NBLK
        C["st"] += STB * NBLK
        self.last_store = (SEM["st"], C["st"])
        self.free_temps()

    def imp_phase(self):
        if not self.want():
            return
        nc, D, g = self.nc, self.D, self.g
        dm, S, T, TOK, DC, W, NTW, HO, NBLK = (g.dm, g.S, g.T, g.TOK, g.DC,
                                               g.W, g.NTW, g.HO, g.NBLK)
        NVIS, TS, TE = g.NVIS, g.TS, g.TE
        SELT = dm * 128
        assert NTW == 2
        SEM, C = self.psems(["ld", "lw", "pe", "ev", "st"])
        self.seed_sems(SEM, C, dma={"st": 32}, eng={"pe": 2 * NTW,
                                                    "ev": 2 * (NTW + 1)})
        L0, S0, P0, E0, LD0 = C["lw"], C["st"], C["pe"], C["ev"], C["ld"]
        PEB, EVB, STB, LWB = HO * NTW, HO * (NTW + 1), HO * 16, HO * 16
        LDB = 48

        with ExitStack() as pes:
            def sb(name, shape, dt):
                return pes.enter_context(nc.sbuf_tensor(name, shape, dt))

            wsb = [sb(f"im_wb{i}", [128, DC * 128], BF16) for i in range(2)]
            c_cos = sb("im_cos", [128, W], F32)
            c_sin = sb("im_sin", [128, W], F32)
            ro1 = sb("im_ro1", [128, TOK], F32)
            rt = sb("im_rt", [128, TOK], F32)
            r1 = sb("im_r1", [128, TOK], F32)
            qrt = sb("im_qrt", [128, TOK], F32)
            qp = [sb(f"im_qp{i}", [128, 1], F32) for i in range(2)]
            qps = sb("im_qps", [128, 1], F32)
            qsp_sb = sb("im_qsp", [128, HO * NBLK], F32)
            qsum_f = sb("im_qsf", [128, HO], F32)
            qsum_bf = sb("im_qsb", [128, HO], BF16)
            rv1 = sb("im_rv1", [128, NVIS], F32)
            rv2 = sb("im_rv2", [128, NVIS], F32)
            rv3 = sb("im_rv3", [128, NVIS], F32)
            kvb = sb("im_kvb", [128, NVIS], BF16)
            kvb2 = sb("im_kv2", [128, NVIS], BF16)
            imp_sb = sb("im_imp", [1, 512], F32)
            ps_m = [pes.enter_context(nc.psum_tensor(f"im_pm{i}",
                                                     [128, TOK], F32))
                    for i in range(4)]
            ps_kv = pes.enter_context(nc.psum_tensor("im_pkv", [128, NVIS], F32))
            ps_imp = pes.enter_context(nc.psum_tensor("im_pim", [1, NVIS], F32))

            def win_sync(sync, b, pe_need, ev_need):
                ld_s = LD0 + b * LDB
                if pe_need is not None:
                    sync.wait_ge(SEM["pe"], pe_need)
                    sync.wait_ge(SEM["ev"], ev_need)
                sync.wait_ge(SEM["ld"], ld_s)
                sync.dma_start(
                    out=self.lnx[:].rearrange("p (c w) -> p c w", c=DC),
                    in_=bass.AP(D["sh_lnx20"], b * W,
                                [[S, 128], [128 * S, DC], [1, W]])
                    ).then_inc(SEM["ld"], 16)
                sync.wait_ge(SEM["ld"], ld_s + 16)
                sync.dma_start(out=c_cos[:],
                               in_=bass.AP(D["cosT"], b * W, [[S, 128], [1, W]])
                               ).then_inc(SEM["ld"], 16)
                sync.wait_ge(SEM["ld"], ld_s + 32)
                sync.dma_start(out=c_sin[:],
                               in_=bass.AP(D["sinT"], b * W, [[S, 128], [1, W]])
                               ).then_inc(SEM["ld"], 16)
                sync.wait_ge(SEM["ld"], ld_s + 48)

            def bases(blk, fc, par):
                B = {}
                B["pe"] = P0 + PEB * blk + fc * NTW
                B["ev"] = E0 + EVB * blk + fc * (NTW + 1)
                B["st"] = S0 + STB * blk + fc * 16
                B["lw"] = L0 + LWB * blk + fc * 16
                B["ld"] = LD0 + (blk + 1) * LDB
                return B

            def tbnds(blk, nt):
                base = blk * W + nt * TOK
                a = max(0, TS - base)
                bnd = min(TOK, TE - base)
                return a, bnd

            def Q_sync(sync, blk, fc, par):
                B = bases(blk, fc, par)
                sync.wait_ge(SEM["pe"], B["pe"] - NTW)
                sync.wait_ge(SEM["lw"], B["lw"])
                sync.dma_start(
                    out=wsb[par][:].rearrange("p (c o) -> p c o", c=DC),
                    in_=bass.AP(D["wqkvb"], (fc * 3) * SELT,
                                [[128, 128], [128 * 128, DC], [1, 128]]),
                ).then_inc(SEM["lw"], 16)

            def Q_vec(vec, blk, fc, par):
                B = bases(blk, fc, par)
                for nt in range(NTW):
                    a, bnd = tbnds(blk, nt)
                    bank = ps_m[2 * par + nt % 2][:]
                    vec.wait_ge(SEM["pe"], B["pe"] + nt + 1)
                    vec.tensor_mul(ro1[:], bank,
                                   c_cos[:, nt * TOK:(nt + 1) * TOK])
                    vec.tensor_scalar_mul(rt[0:64, :],
                                          ps_m[2 * par + nt % 2][64:128, :],
                                          -1.0)
                    vec.tensor_copy(rt[64:128, :],
                                    ps_m[2 * par + nt % 2][0:64, :])
                    vec.drain()
                    vec.tensor_mul(r1[:], rt[:],
                                   c_sin[:, nt * TOK:(nt + 1) * TOK])
                    vec.drain()
                    vec.tensor_add(qrt[:], ro1[:], r1[:])
                    vec.drain().then_inc(SEM["ev"], 1)
                    if bnd > a:
                        vec.tensor_reduce(qp[nt][:], qrt[:, a:bnd],
                                          mybir.AxisListType.X, ALU.add)
                    else:
                        vec.memset(qp[nt][:], 0.0)
                    vec.drain()
                vec.wait_ge(SEM["st"], B["st"])
                vec.tensor_add(qps[:], qp[0][:], qp[1][:])
                vec.drain().then_inc(SEM["ev"], 1)

            def Q_pe(te, blk, fc, par):
                B = bases(blk, fc, par)
                te.wait_ge(SEM["lw"], B["lw"] + 16)
                te.wait_ge(SEM["ev"], B["ev"] - 2 * (NTW + 1) + NTW)
                te.wait_ge(SEM["ld"], B["ld"])
                for nt in range(NTW):
                    bank = ps_m[2 * par + nt % 2][:]
                    for hc in range(DC):
                        mm = te.matmul(bank, wsb[par][:, hc * 128:(hc + 1) * 128],
                                       self.lnx[:, hc * W + nt * TOK:
                                                hc * W + nt * TOK + TOK],
                                       start=(hc == 0), stop=(hc == DC - 1))
                    mm.then_inc(SEM["pe"], 1)

            def Q_gp(gp, blk, fc, par):
                B = bases(blk, fc, par)
                gp.wait_ge(SEM["ev"], B["ev"] + NTW + 1)
                gp.wait_ge(SEM["st"], B["st"])
                gp.dma_start(
                    out=bass.AP(D["qs"], (fc * NBLK + blk) * 128,
                                [[1, 128], [1, 1]]),
                    in_=qps[:]).then_inc(SEM["st"], 16)

            for b in range(NBLK):
                with nc.Block(f"imp_q{b}") as blk:
                    @blk.sync
                    def _(sync, b=b):
                        win_sync(sync, b, None if b == 0 else P0 + PEB * b,
                                 E0 + EVB * b)
                        with sync.Fori(0, HO // 2) as it:
                            Q_sync(sync, b, it * 2, 0)
                            Q_sync(sync, b, it * 2 + 1, 1)

                    @blk.vector
                    def _(vec, b=b):
                        with vec.Fori(0, HO // 2) as it:
                            Q_vec(vec, b, it * 2, 0)
                            Q_vec(vec, b, it * 2 + 1, 1)

                    @blk.tensor
                    def _(te, b=b):
                        with te.Fori(0, HO // 2) as it:
                            Q_pe(te, b, it * 2, 0)
                            Q_pe(te, b, it * 2 + 1, 1)

                    @blk.gpsimd
                    def _(gp, b=b):
                        with gp.Fori(0, HO // 2) as it:
                            Q_gp(gp, b, it * 2, 0)
                            Q_gp(gp, b, it * 2 + 1, 1)
                self.free_temps()
            with nc.Block("imp_qe") as blk:
                @blk.gpsimd
                def _(gp):
                    gp.wait_ge(SEM["st"], S0 + STB * NBLK)

            C["ld"] += NBLK * LDB
            C["lw"] += LWB * NBLK
            C["pe"] += PEB * NBLK
            C["ev"] += EVB * NBLK
            C["st"] += STB * NBLK

            ld1, ev1, pe1, st1, lw1 = (C[k] for k in
                                       ["ld", "ev", "pe", "st", "lw"])
            with nc.Block("imp_as") as blk:
                @blk.sync
                def _(sync):
                    sync.wait_ge(SEM["st"], st1)
                    sync.dma_start(out=qsp_sb[:],
                                   in_=bass.AP(D["qs"], 0,
                                               [[1, 128], [128, HO * NBLK]]),
                                   allow_slow_non_contiguous=True
                                   ).then_inc(SEM["ld"], 16)
                    sync.wait_ge(SEM["ld"], ld1 + 16)
                    sync.wait_ge(SEM["pe"], pe1)
                    win_sync(sync, 0, None, None)

                @blk.vector
                def _(vec):
                    vec.wait_ge(SEM["ld"], ld1 + 16)
                    for fc in range(HO):
                        vec.tensor_reduce(qsum_f[:, fc:fc + 1],
                                          qsp_sb[:, fc * NBLK:(fc + 1) * NBLK],
                                          mybir.AxisListType.X, ALU.add)
                    vec.drain()
                    vec.tensor_copy(qsum_bf[:], qsum_f[:])
                    vec.drain().then_inc(SEM["ev"], 1)
            C["ld"] += 16 + LDB
            C["ev"] += 1
            ld2, ev2, pe2, st2, lw2 = (C[k] for k in
                                       ["ld", "ev", "pe", "st", "lw"])

            def K_sync(sync, fc):
                sync.wait_ge(SEM["pe"], pe2 + fc)
                sync.wait_ge(SEM["lw"], lw2 + fc * 16)
                sync.dma_start(
                    out=wsb[0][:].rearrange("p (c o) -> p c o", c=DC),
                    in_=bass.AP(D["wqkvb"], (fc * 3 + 1) * SELT,
                                [[128, 128], [128 * 128, DC], [1, 128]]),
                ).then_inc(SEM["lw"], 16)

            def K_pe(te, fc):
                te.wait_ge(SEM["lw"], lw2 + fc * 16 + 16)
                te.wait_ge(SEM["ev"], ev2 + fc)
                te.wait_ge(SEM["ld"], ld2)
                for hc in range(DC):
                    mm = te.matmul(ps_kv[:], wsb[0][:, hc * 128:(hc + 1) * 128],
                                   self.lnx[:, hc * W + 1:hc * W + 1 + NVIS],
                                   start=(hc == 0), stop=(hc == DC - 1))
                mm.then_inc(SEM["pe"], 1)

            def K_vec(vec, fc):
                vec.wait_ge(SEM["pe"], pe2 + fc + 1)
                vec.wait_ge(SEM["st"], st2 + fc * 16)
                vec.tensor_mul(rv1[:], ps_kv[:], c_cos[:, 1:1 + NVIS])
                vec.tensor_scalar_mul(rv2[0:64, :], ps_kv[64:128, :], -1.0)
                vec.tensor_copy(rv2[64:128, :], ps_kv[0:64, :])
                vec.drain()
                vec.tensor_mul(rv3[:], rv2[:], c_sin[:, 1:1 + NVIS])
                vec.drain()
                vec.tensor_add(kvb[:], rv1[:], rv3[:])
                vec.drain().then_inc(SEM["ev"], 1)

            def K_gp(gp, fc):
                gp.wait_ge(SEM["ev"], ev2 + fc + 1)
                gp.wait_ge(SEM["st"], st2 + fc * 16)
                gp.dma_start(out=bass.AP(D["kvis"], fc * (128 * 512),
                                         [[512, 128], [1, NVIS]]),
                             in_=kvb[:]).then_inc(SEM["st"], 16)

            with nc.Block("imp_kv") as blk:
                @blk.sync
                def _(sync):
                    with sync.Fori(0, HO) as fc:
                        K_sync(sync, fc)

                @blk.tensor
                def _(te):
                    with te.Fori(0, HO) as fc:
                        K_pe(te, fc)

                @blk.vector
                def _(vec):
                    with vec.Fori(0, HO) as fc:
                        K_vec(vec, fc)

                @blk.gpsimd
                def _(gp):
                    with gp.Fori(0, HO) as fc:
                        K_gp(gp, fc)
                    gp.wait_ge(SEM["st"], st2 + HO * 16)
            C["lw"] += HO * 16
            C["pe"] += HO
            C["ev"] += HO
            C["st"] += HO * 16
            ld3, ev3, pe3, st3 = (C[k] for k in ["ld", "ev", "pe", "st"])

            with nc.Block("imp_acc") as blk:
                @blk.sync
                def _(sync):
                    for fc in range(HO):
                        sync.wait_ge(SEM["ld"], ld3 + fc * 16)
                        if fc > 0:
                            sync.wait_ge(SEM["pe"], pe3 + fc)
                        else:
                            sync.wait_ge(SEM["st"], st3)
                        sync.dma_start(out=kvb2[:],
                                       in_=bass.AP(D["kvis"], fc * (128 * 512),
                                                   [[512, 128], [1, NVIS]])
                                       ).then_inc(SEM["ld"], 16)

                @blk.tensor
                def _(te):
                    for fc in range(HO):
                        te.wait_ge(SEM["ld"], ld3 + (fc + 1) * 16)
                        te.matmul(ps_imp[:], qsum_bf[:, fc:fc + 1], kvb2[:],
                                  start=(fc == 0), stop=(fc == HO - 1)
                                  ).then_inc(SEM["pe"], 1)

                @blk.vector
                def _(vec):
                    vec.wait_ge(SEM["pe"], pe3 + HO)
                    vec.memset(imp_sb[:], -1e30)
                    vec.drain()
                    vec.tensor_copy(imp_sb[:, 0:NVIS], ps_imp[:])
                    vec.drain().then_inc(SEM["ev"], 1)

                @blk.gpsimd
                def _(gp):
                    gp.wait_ge(SEM["ev"], ev3 + 1)
                    gp.wait_ge(SEM["st"], st3)
                    with gp.If_eq(self.rankreg["g"], 0):
                        gp.dma_start(out=bass.AP(D["sh_imp"], 0, [[1, 512]]),
                                     in_=imp_sb[:]).then_inc(SEM["st"], 16)
                    with gp.Else():
                        gp.dma_start(out=bass.AP(D["sh_imp"], 512, [[1, 512]]),
                                     in_=imp_sb[:]).then_inc(SEM["st"], 16)
                    gp.wait_ge(SEM["st"], st3 + 16)
            C["ld"] += HO * 16
            C["pe"] += HO
            C["ev"] += 1
            C["st"] += 16
        self.last_store = (SEM["st"], C["st"])
        self.free_temps()

    def rank_phase(self):
        if not self.want():
            return
        nc, D, g = self.nc, self.D, self.g
        NVIS, NKEEP = g.NVIS, g.NKEEP
        SEM, C = self.psems(["ld", "pe", "ev", "st"])
        with ExitStack() as pes:
            def sb(name, shape, dt):
                return pes.enter_context(nc.sbuf_tensor(name, shape, dt))

            imp_a = sb("rk_ia", [1, 512], F32)
            imp_b = sb("rk_ib", [1, 512], F32)
            imp_row = sb("rk_ir", [1, 512], F32)
            icol = sb("rk_ic", [128, 4], F32)
            cmp_bf = sb("rk_cm", [128, 512], BF16)
            krow = sb("rk_kr", [1, 512], F32)
            drow = sb("rk_dr", [1, 512], F32)
            dcol_f = sb("rk_dcf", [128, 4], F32)
            dcol = sb("rk_dc", [128, 4], BF16)
            srow_sb = sb("rk_sr", [1, 4], F32)
            run = sb("rk_run", [128, 1], F32)
            ps_b1 = pes.enter_context(nc.psum_tensor("rk_pb1", [128, 512], F32))
            ps_rank = pes.enter_context(nc.psum_tensor("rk_prk", [1, 512], F32))
            ps_cnt = pes.enter_context(nc.psum_tensor("rk_pcn", [128, 4], F32))
            ps_sr = pes.enter_context(nc.psum_tensor("rk_psr", [1, 4], F32))
            ps_bc = pes.enter_context(nc.psum_tensor("rk_pbc", [128, 4], F32))
            ps_kb = pes.enter_context(nc.psum_tensor("rk_pkb", [128, 512], F32))

            with nc.Block("rank") as blk:
                @blk.sync
                def _(sync):
                    sync.dma_start(out=imp_a[:],
                                   in_=bass.AP(D["sh_imp"], 0, [[1, 512]])
                                   ).then_inc(SEM["ld"], 16)
                    sync.wait_ge(SEM["ld"], 16)
                    sync.dma_start(out=imp_b[:],
                                   in_=bass.AP(D["sh_imp"], 512, [[1, 512]])
                                   ).then_inc(SEM["ld"], 16)
                    sync.wait_ge(SEM["ld"], 32)
                    sync.wait_ge(SEM["st"], 16)
                    sync.dma_start(out=icol[:],
                                   in_=bass.AP(D["rowb"], 0,
                                               [[1, 128], [128, 4]]),
                                   allow_slow_non_contiguous=True
                                   ).then_inc(SEM["ld"], 16)
                    sync.wait_ge(SEM["ld"], 48)
                    sync.wait_ge(SEM["st"], 32)
                    sync.dma_start(out=dcol_f[:],
                                   in_=bass.AP(D["rowb"], 0,
                                               [[1, 128], [128, 4]]),
                                   allow_slow_non_contiguous=True
                                   ).then_inc(SEM["ld"], 16)
                    sync.wait_ge(SEM["ld"], 64)

                @blk.vector
                def _(vec):
                    vec.wait_ge(SEM["ld"], 32)
                    vec.tensor_add(imp_row[:], imp_a[:], imp_b[:])
                    vec.drain().then_inc(SEM["ev"], 1)
                    for c in range(4):
                        vec.wait_ge(SEM["pe"], 1)
                        vec.wait_ge(SEM["ld"], 48)
                        if c > 0:
                            vec.wait_ge(SEM["pe"], 1 + c)
                        vec.tensor_scalar(cmp_bf[:], ps_b1[:],
                                          icol[:, c:c + 1], None, ALU.is_lt)
                        vec.drain().then_inc(SEM["ev"], 1)
                    vec.wait_ge(SEM["pe"], 5)
                    vec.tensor_scalar(krow[:], ps_rank[:], NKEEP - 0.5, None,
                                      ALU.is_lt)
                    vec.tensor_scalar(drow[:], ps_rank[:], NKEEP - 0.5, None,
                                      ALU.is_ge)
                    vec.drain()
                    if NVIS < 512:
                        vec.memset(drow[:, NVIS:512], 0.0)
                        vec.drain()
                    vec.drain().then_inc(SEM["ev"], 1)
                    vec.wait_ge(SEM["ld"], 64)
                    vec.tensor_copy(dcol[:], dcol_f[:])
                    vec.drain().then_inc(SEM["ev"], 1)
                    vec.wait_ge(SEM["pe"], 7)
                    vec.tensor_copy(srow_sb[:], ps_sr[:])
                    vec.drain().then_inc(SEM["ev"], 1)
                    vec.wait_ge(SEM["pe"], 8)
                    vec.tensor_copy(self.cnt_cols[:, 0:1], ps_cnt[:, 0:1])
                    vec.tensor_copy(run[:], ps_bc[:, 0:1])
                    vec.drain()
                    vec.tensor_add(self.cnt_cols[:, 1:2], ps_cnt[:, 1:2], run[:])
                    vec.tensor_add(run[:], run[:], ps_bc[:, 1:2])
                    vec.drain()
                    vec.tensor_add(self.cnt_cols[:, 2:3], ps_cnt[:, 2:3], run[:])
                    vec.tensor_add(run[:], run[:], ps_bc[:, 2:3])
                    vec.drain()
                    vec.tensor_add(self.cnt_cols[:, 3:4], ps_cnt[:, 3:4], run[:])
                    vec.tensor_add(self.cnt_tot[:], run[:], ps_bc[:, 3:4])
                    vec.drain().then_inc(SEM["ev"], 1)
                    vec.wait_ge(SEM["pe"], 9)
                    vec.tensor_copy(self.keep_bc[:], ps_kb[:])
                    vec.drain().then_inc(SEM["ev"], 1)

                @blk.tensor
                def _(te):
                    te.wait_ge(SEM["ev"], 1)
                    te.matmul(ps_b1[:], self.c_onesr_f[:], imp_row[:],
                              start=True, stop=True).then_inc(SEM["pe"], 1)
                    for c in range(4):
                        te.wait_ge(SEM["ev"], 2 + c)
                        te.matmul(ps_rank[:], self.c_ones_bf[:], cmp_bf[:],
                                  start=(c == 0), stop=(c == 3)
                                  ).then_inc(SEM["pe"], 1)
                    te.wait_ge(SEM["ev"], 7)
                    for c in range(4):
                        te.matmul(ps_cnt[:, c:c + 1], self.c_tri[:],
                                  dcol[:, c:c + 1], start=True, stop=True)
                    te.matmul(ps_sr[:], self.c_ones_bf[:], dcol[:],
                              start=True, stop=True).then_inc(SEM["pe"], 2)
                    te.wait_ge(SEM["ev"], 8)
                    te.matmul(ps_bc[:], self.c_onesr_f[:], srow_sb[:],
                              start=True, stop=True).then_inc(SEM["pe"], 1)
                    te.wait_ge(SEM["ev"], 6)
                    te.matmul(ps_kb[:], self.c_onesr_f[:], krow[:],
                              start=True, stop=True).then_inc(SEM["pe"], 1)

                @blk.gpsimd
                def _(gp):
                    gp.wait_ge(SEM["ev"], 1)
                    gp.dma_start(out=bass.AP(D["rowb"], 0, [[1, 512]]),
                                 in_=imp_row[:]).then_inc(SEM["st"], 16)
                    gp.wait_ge(SEM["st"], 16)
                    gp.wait_ge(SEM["ev"], 6)
                    gp.dma_start(out=bass.AP(D["rowb"], 0, [[1, 512]]),
                                 in_=drow[:]).then_inc(SEM["st"], 16)
                    gp.wait_ge(SEM["st"], 32)
        self.free_temps()

    def zero_kv_phase(self):
        if not self.want():
            return
        nc, D, g = self.nc, self.D, self.g
        S, HO, NVIS = g.S, g.HO, g.NVIS
        SEM, C = self.psems(["ld", "ev", "st"])
        self.seed_sems(SEM, C, dma={"st": 32}, eng={"ev": 2})
        E0, S0, LD0 = C["ev"], C["st"], C["ld"]
        with ExitStack() as pes:
            def sb(name, shape, dt):
                return pes.enter_context(nc.sbuf_tensor(name, shape, dt))

            zt = [sb(f"zk_t{i}", [128, NVIS], BF16) for i in range(2)]
            zo = [sb(f"zk_o{i}", [128, NVIS], BF16) for i in range(2)]
            SRCS = [D["kb"], D["vb"]]

            with nc.Block("zkv") as blk:
                @blk.sync
                def _(sync):
                    with sync.Fori(0, HO) as fc:
                        for t in range(2):
                            sync.wait_ge(SEM["ev"], E0 + fc * 2 - 2 + t + 1)
                            sync.wait_ge(SEM["ld"], LD0 + (fc * 2 + t) * 16)
                            sync.dma_start(
                                out=zt[t][:],
                                in_=bass.AP(SRCS[t], fc * (128 * S) + 1,
                                            [[S, 128], [1, NVIS]])
                                ).then_inc(SEM["ld"], 16)

                @blk.vector
                def _(vec):
                    with vec.Fori(0, HO) as fc:
                        for t in range(2):
                            vec.wait_ge(SEM["ld"], LD0 + (fc * 2 + t + 1) * 16)
                            vec.wait_ge(SEM["st"], S0 + (fc * 2 + t - 2) * 16
                                        + 16)
                            vec.tensor_mul(zo[t][:], zt[t][:],
                                           self.keep_bc[:, 0:NVIS])
                            vec.drain().then_inc(SEM["ev"], 1)

                @blk.gpsimd
                def _(gp):
                    with gp.Fori(0, HO) as fc:
                        for t in range(2):
                            gp.wait_ge(SEM["ev"], E0 + fc * 2 + t + 1)
                            gp.wait_ge(SEM["st"], S0 + (fc * 2 + t) * 16)
                            gp.dma_start(
                                out=bass.AP(SRCS[t], fc * (128 * S) + 1,
                                            [[S, 128], [1, NVIS]]),
                                in_=zo[t][:]).then_inc(SEM["st"], 16)
                    gp.wait_ge(SEM["st"], S0 + HO * 32)
        C["ld"] += HO * 32
        C["ev"] += HO * 2
        C["st"] += HO * 32
        self.last_store = (SEM["st"], C["st"])
        self.free_temps()

    def attn_phase(self):
        if not self.want():
            return
        nc, D, g = self.nc, self.D, self.g
        S, HO, NG, SCALE = g.S, g.HO, g.NG, g.SCALE
        SEM, C = self.psems(["ld", "pe", "aq", "ev", "st"])

        with ExitStack() as pes:
            def sb(name, shape, dt):
                return pes.enter_context(nc.sbuf_tensor(name, shape, dt))

            q_sb = sb("at_q", [128, 512], BF16)
            qsq = sb("at_qsq", [128, 512], BF16)
            kblk = sb("at_kb", [128, 512], BF16)
            vblk = sb("at_vb", [128, 512], BF16)
            vt_sb = sb("at_vt", [128, 512], BF16)
            pt_sb = sb("at_pt", [128, 512], BF16)
            p_sb = [sb(f"at_p{i}", [128, 512], BF16) for i in range(4)]
            mskf = [sb(f"at_mk{i}", [128, 512], F32) for i in range(2)]
            u_row = sb("at_u", [1, 512], F32)
            v_row = sb("at_v", [1, 512], F32)
            w_row = sb("at_w", [1, 512], F32)
            nrow = sb("at_nr", [1, 512], F32)
            nbc = sb("at_nbc", [128, 4], F32)
            eb = sb("at_eb", [128, 4], F32)
            l_acc = sb("at_l", [128, 4], F32)
            lch = sb("at_lch", [128, 4], F32)
            corr = sb("at_cor", [128, 4], F32)
            rl = sb("at_rl", [128, 4], F32)
            rlr_sb = sb("at_rlr", [1, 512], F32)
            rlb_sb = sb("at_rlb", [128, 512], F32)
            at_bf = sb("at_o", [128, 512], BF16)
            km = sb("at_km", [1, 1], F32)
            kc = sb("at_kc", [1, 1], F32)
            ps_qk = [pes.enter_context(nc.psum_tensor(f"at_pqk{i}",
                                                      [128, 512], F32))
                     for i in range(4)]
            ps_pt = pes.enter_context(nc.psum_tensor("at_ppt", [128, 512], BF16))
            ps_vt = pes.enter_context(nc.psum_tensor("at_pvt", [128, 512], BF16))
            ps_ot = pes.enter_context(nc.psum_tensor("at_pot", [128, 512], F32))
            ps_ms = pes.enter_context(nc.psum_tensor("at_pms", [128, 512], F32))

            FBLK = dict(ld=32, pe=7, aq=9, ev=1, st=0)
            BBLK = dict(ld=32, pe=7, aq=9, ev=5, st=0)

            with nc.Block("atkm") as blk:
                @blk.sync
                def _(sync):
                    with sync.Fori(0, HO) as fc:
                        with sync.Fori(0, NG) as b:
                            sync.wait_ge(SEM["aq"], fc * NG + b)
                            sync.wait_ge(SEM["ld"], (fc * NG + b) * 16)
                            sync.dma_start(
                                out=kblk[:],
                                in_=bass.AP(D["kb"], fc * (128 * S) + b * 512,
                                            [[S, 128], [1, 512]])
                                ).then_inc(SEM["ld"], 16)

                @blk.scalar
                def _(sc):
                    with sc.Fori(0, HO) as fc:
                        with sc.Fori(0, NG) as b:
                            sc.wait_ge(SEM["ld"], (fc * NG + b) * 16 + 16)
                            sc.wait_ge(SEM["pe"], fc * NG + b)
                            sc.activation(p_sb[0][:], kblk[:], AF.Square
                                          ).then_inc(SEM["aq"], 1)

                @blk.tensor
                def _(te):
                    with te.Fori(0, HO) as fc:
                        with te.Fori(0, NG) as b:
                            te.wait_ge(SEM["aq"], fc * NG + b + 1)
                            te.wait_ge(SEM["ev"], fc * NG + b)
                            te.matmul(ps_ms[0:1, 0:512], self.c_ones_bf[:],
                                      p_sb[0][:], start=True, stop=True
                                      ).then_inc(SEM["pe"], 1)

                @blk.vector
                def _(vec):
                    vec.memset(km[:], -1e30)
                    vec.drain()
                    with vec.Fori(0, HO) as fc:
                        with vec.Fori(0, NG) as b:
                            vec.wait_ge(SEM["pe"], fc * NG + b + 1)
                            vec.tensor_reduce(kc[:], ps_ms[0:1, 0:512],
                                              mybir.AxisListType.X, ALU.max)
                            vec.drain()
                            vec.tensor_max(km[:], km[:], kc[:])
                            vec.drain().then_inc(SEM["ev"], 1)
            C["ld"] += HO * NG * 16
            C["aq"] += HO * NG
            C["pe"] += HO * NG
            C["ev"] += HO * NG
            self.free_temps()

            def unit(eng, kind, hb, h, gg):
                O = {k: 0 for k in ["ld", "pe", "aq", "ev", "st"]}

                def B(sem, extra=0):
                    return hb[sem] + O[sem] + extra

                E = kind
                if E == "sync":
                    eng.wait_ge(SEM["pe"], hb["pe"])
                    eng.wait_ge(SEM["aq"], hb["aq"])
                    eng.wait_ge(SEM["ld"], B("ld"))
                    eng.dma_start(
                        out=q_sb[:],
                        in_=bass.AP(D["qb"], h * (128 * S) + gg * 512,
                                    [[S, 128], [1, 512]])
                        ).then_inc(SEM["ld"], 16)
                O["ld"] += 16
                if E == "act":
                    eng.wait_ge(SEM["ld"], B("ld"))
                    eng.wait_ge(SEM["pe"], hb["pe"])
                    eng.activation(qsq[:], q_sb[:], AF.Square
                                   ).then_inc(SEM["aq"], 1)
                O["aq"] += 1
                if E == "te":
                    eng.wait_ge(SEM["aq"], B("aq"))
                    eng.wait_ge(SEM["ev"], hb["ev"])
                    eng.matmul(ps_ms[0:1, 0:512], self.c_ones_bf[:], qsq[:],
                               start=True, stop=True).then_inc(SEM["pe"], 1)
                O["pe"] += 1
                if E == "vec":
                    eng.wait_ge(SEM["pe"], B("pe"))
                    eng.tensor_scalar(u_row[:], ps_ms[0:1, 0:512],
                                      km[0:1, 0:1], None, ALU.mult)
                    eng.drain().then_inc(SEM["ev"], 1)
                O["ev"] += 1
                if E == "act":
                    eng.wait_ge(SEM["ev"], B("ev"))
                    eng.activation(v_row[:], u_row[:], AF.Ln)
                    eng.drain()
                    eng.activation(w_row[:], v_row[:], AF.Exp, scale=0.5)
                    eng.drain().then_inc(SEM["aq"], 1)
                O["aq"] += 1
                if E == "vec":
                    eng.wait_ge(SEM["aq"], B("aq"))
                    eng.tensor_scalar(nrow[:], w_row[:], -SCALE, None,
                                      ALU.mult)
                    eng.drain().then_inc(SEM["ev"], 1)
                O["ev"] += 1
                if E == "te":
                    eng.wait_ge(SEM["ev"], B("ev"))
                    for i in range(4):
                        mm = eng.matmul(ps_ms[:, i:i + 1],
                                        nrow[0:1, i * 128:(i + 1) * 128],
                                        self.c_one_f[:], start=True, stop=True)
                    mm.then_inc(SEM["pe"], 1)
                O["pe"] += 1
                if E == "vec":
                    eng.wait_ge(SEM["pe"], B("pe"))
                    eng.tensor_copy(nbc[:], ps_ms[:, 0:4])
                    eng.drain().then_inc(SEM["ev"], 1)
                O["ev"] += 1
                if E == "act":
                    eng.wait_ge(SEM["ev"], B("ev"))
                    eng.activation(eb[:], nbc[:], AF.Exp).then_inc(SEM["aq"], 1)
                O["aq"] += 1
                if E == "vec":
                    eng.memset(l_acc[:], 0.0)
                    eng.drain()

                BH = dict(O)

                def block(bofs, b, first_of_g, boundary):
                    if E == "sync":
                        eng.wait_ge(SEM["pe"], bofs["pe"])
                        eng.wait_ge(SEM["ld"], bofs["ld"])
                        eng.dma_start(
                            out=kblk[:],
                            in_=bass.AP(D["kb"], h * (128 * S) + b * 512,
                                        [[S, 128], [1, 512]])
                            ).then_inc(SEM["ld"], 16)
                        eng.wait_ge(SEM["ld"], bofs["ld"] + 16)
                        eng.dma_start(
                            out=vblk[:],
                            in_=bass.AP(D["vb"], h * (128 * S) + b * 512,
                                        [[S, 128], [1, 512]])
                            ).then_inc(SEM["ld"], 16)
                    if E == "te":
                        eng.wait_ge(SEM["ld"], bofs["ld"] + 32)
                        eng.wait_ge(SEM["aq"], bofs["aq"])
                        for c in range(4):
                            tr = eng.transpose(
                                ps_vt[:, c * 128:(c + 1) * 128],
                                vblk[:, c * 128:(c + 1) * 128],
                                self.c_id[:])
                        tr.then_inc(SEM["pe"], 1)
                    if E == "act":
                        eng.wait_ge(SEM["pe"], bofs["pe"] + 1)
                        eng.activation(vt_sb[:], ps_vt[:], AF.Copy)
                        eng.drain().then_inc(SEM["aq"], 1)
                    if E == "te":
                        for i in range(4):
                            mm = eng.matmul(
                                ps_qk[i][:],
                                q_sb[:, i * 128:(i + 1) * 128],
                                kblk[:], start=True, stop=True)
                        mm.then_inc(SEM["pe"], 1)
                    if boundary:
                        for i in range(4):
                            if E == "vec":
                                eng.wait_ge(SEM["pe"], bofs["pe"] + 2)
                                if i >= 2:
                                    eng.wait_ge(SEM["aq"], bofs["aq"] + i)
                                eng.tensor_add(
                                    mskf[i % 2][:], ps_qk[i][:],
                                    self.c_cmask[:, i * 512:(i + 1) * 512])
                                eng.drain().then_inc(SEM["ev"], 1)
                            if E == "act":
                                if i == 0:
                                    eng.wait_ge(SEM["ev"], bofs["ev"])
                                eng.wait_ge(SEM["ev"], bofs["ev"] + 1 + i)
                                eng.activation(
                                    p_sb[i][:], mskf[i % 2][:], AF.Exp,
                                    bias=nbc[:, i:i + 1], scale=SCALE,
                                    accum_out=lch[:, i:i + 1]
                                    ).then_inc(SEM["aq"], 1)
                        evl = 6
                    else:
                        if E == "act":
                            eng.wait_ge(SEM["pe"], bofs["pe"] + 2)
                            eng.wait_ge(SEM["ev"], bofs["ev"])
                            for i in range(4):
                                eng.activation(
                                    p_sb[i][:], ps_qk[i][:], AF.Exp,
                                    bias=nbc[:, i:i + 1], scale=SCALE,
                                    accum_out=lch[:, i:i + 1]
                                    ).then_inc(SEM["aq"], 1)
                        evl = 2
                    if E == "vec":
                        eng.wait_ge(SEM["aq"], bofs["aq"] + 5)
                        eng.tensor_add(l_acc[:], l_acc[:], lch[:])
                        eng.drain().then_inc(SEM["ev"], 1)
                    for c in range(4):
                        if E == "te":
                            if c == 0:
                                eng.wait_ge(SEM["aq"], bofs["aq"] + 5)
                            else:
                                eng.wait_ge(SEM["aq"], bofs["aq"] + 5 + c)
                            for i in range(4):
                                tr = eng.transpose(
                                    ps_pt[:, i * 128:(i + 1) * 128],
                                    p_sb[i][:, c * 128:(c + 1) * 128],
                                    self.c_id[:])
                            tr.then_inc(SEM["pe"], 1)
                        if E == "act":
                            eng.wait_ge(SEM["pe"], bofs["pe"] + 3 + c)
                            eng.activation(pt_sb[:], ps_pt[:], AF.Copy)
                            eng.drain().then_inc(SEM["aq"], 1)
                        if E == "te":
                            eng.wait_ge(SEM["aq"], bofs["aq"] + 6 + c)
                            mm = eng.matmul(
                                ps_ot[:], vt_sb[:, c * 128:(c + 1) * 128],
                                pt_sb[:],
                                start=(first_of_g and c == 0),
                                stop=(boundary and c == 3))
                            if c == 3:
                                mm.then_inc(SEM["pe"], 1)

                def blk_bases(b):
                    return {k: hb[k] + BH[k] + FBLK[k] * b for k in FBLK}

                if gg == 0:
                    block(blk_bases(0), 0, True, True)
                    for k in BBLK:
                        O[k] += BBLK[k]
                else:
                    block(blk_bases(0), 0, True, False)
                    for k in FBLK:
                        O[k] += FBLK[k]
                    if gg >= 2:
                        if E in ("sync", "te", "act", "vec"):
                            with eng.Fori(1, gg) as b:
                                block(blk_bases(b), b, False, False)
                        for k in FBLK:
                            O[k] += FBLK[k] * (gg - 1)
                    bb = {k: hb[k] + BH[k] + FBLK[k] * gg for k in FBLK}
                    block(bb, gg, False, True)
                    for k in BBLK:
                        O[k] += BBLK[k]

                if E == "vec":
                    eng.wait_ge(SEM["aq"], B("aq"))
                    if gg == 0:
                        eng.tensor_mul(corr[:], eb[:], self.cnt_cols[:])
                    else:
                        eng.tensor_scalar(corr[:], eb[:],
                                          self.cnt_tot[:, 0:1], None,
                                          ALU.mult)
                    eng.drain()
                    eng.tensor_sub(l_acc[:], l_acc[:], corr[:])
                    eng.drain().then_inc(SEM["ev"], 1)
                O["ev"] += 1
                if E == "act":
                    eng.wait_ge(SEM["ev"], B("ev"))
                    eng.activation(corr[:], l_acc[:], AF.Ln)
                    eng.drain()
                    eng.activation(rl[:], corr[:], AF.Exp, scale=-1.0)
                    eng.drain().then_inc(SEM["aq"], 1)
                O["aq"] += 1
                if E == "te":
                    eng.wait_ge(SEM["aq"], B("aq"))
                    for i in range(4):
                        mm = eng.matmul(ps_ms[0:1, i * 128:(i + 1) * 128],
                                        rl[:, i:i + 1], self.c_id_f[:],
                                        start=True, stop=True)
                    mm.then_inc(SEM["pe"], 1)
                O["pe"] += 1
                if E == "vec":
                    eng.wait_ge(SEM["pe"], B("pe"))
                    eng.tensor_copy(rlr_sb[:], ps_ms[0:1, 0:512])
                    eng.drain().then_inc(SEM["ev"], 1)
                O["ev"] += 1
                if E == "te":
                    eng.wait_ge(SEM["ev"], B("ev"))
                    eng.matmul(ps_ms[:], self.c_onesr_f[:], rlr_sb[:],
                               start=True, stop=True).then_inc(SEM["pe"], 1)
                O["pe"] += 1
                if E == "vec":
                    eng.wait_ge(SEM["pe"], B("pe"))
                    eng.wait_ge(SEM["st"], B("st"))
                    eng.tensor_copy(rlb_sb[:], ps_ms[:])
                    eng.drain()
                    eng.tensor_mul(at_bf[:], ps_ot[:], rlb_sb[:])
                    eng.drain().then_inc(SEM["ev"], 1)
                O["ev"] += 1
                if E == "gp":
                    eng.wait_ge(SEM["ev"], B("ev"))
                    eng.wait_ge(SEM["st"], B("st"))
                    eng.dma_start(
                        out=bass.AP(D["at"], h * (128 * S) + gg * 512,
                                    [[S, 128], [1, 512]]),
                        in_=at_bf[:]).then_inc(SEM["st"], 16)
                O["st"] += 16
                return O

            KINDS = [("sync", "sync"), ("vector", "vec"), ("scalar", "act"),
                     ("tensor", "te"), ("gpsimd", "gp")]
            CGS, GTS = [], []
            for gg in range(NG):
                CGS.append(dict(C))
                with nc.Block(f"at_g{gg}") as blk:
                    def mk(kind, gg=gg):
                        def emit(eng):
                            t = unit(eng, kind, dict(CGS[gg]), 0, gg)
                            if len(GTS) <= gg:
                                GTS.append(t)
                            else:
                                assert t == GTS[gg], (kind, gg, t, GTS[gg])
                            if HO > 1:
                                with eng.Fori(1, HO) as h:
                                    hb = {k: CGS[gg][k] + h * GTS[gg][k]
                                          for k in GTS[gg]}
                                    unit(eng, kind, hb, h, gg)
                        return emit

                    blk.sync(mk("sync"))
                    blk.vector(mk("vec"))
                    blk.scalar(mk("act"))
                    blk.tensor(mk("te"))
                    blk.gpsimd(mk("gp"))
                for k in GTS[gg]:
                    C[k] = CGS[gg][k] + HO * GTS[gg][k]
                self.free_temps()

            with nc.Block("at_end") as blk:
                @blk.gpsimd
                def _(gp):
                    gp.wait_ge(SEM["st"], C["st"])
        self.last_store = (SEM["st"], C["st"])
        self.free_temps()

    def oproj_phase(self):
        if not self.want():
            return
        nc, D, g = self.nc, self.D, self.g
        dm, S, TOK, DC, HO = g.dm, g.S, g.TOK, g.DC, g.HO
        SW = S // 2
        NT2 = SW // TOK
        assert NT2 % 2 == 0 or NT2 == 1
        SEM, C = self.psems(["ld", "lw", "cast", "pe", "ev", "st"])
        self.seed_sems(SEM, C, dma={"st": NT2 * 16 * 2},
                       eng={"pe": 2 * NT2, "ev": 2 * NT2, "cast": 2})
        L0, S0, P0, E0, CA0, LD0 = (C["lw"], C["st"], C["pe"], C["ev"],
                                    C["cast"], C["ld"])
        PEB, STB = DC * NT2, DC * NT2 * 16

        with ExitStack() as pes:
            def sb(name, shape, dt):
                return pes.enter_context(nc.sbuf_tensor(name, shape, dt))

            wsf = sb("op_wf", [128, HO * 128], F32)
            wob = [sb(f"op_wb{i}", [128, HO * 128], BF16) for i in range(2)]
            est = [sb(f"op_e{i}", [128, TOK], BF16) for i in range(2)]
            ps_m = [pes.enter_context(nc.psum_tensor(f"op_pm{i}",
                                                     [128, TOK], F32))
                    for i in range(4)]

            for sp in range(2):
                awin = self.lnx
                with nc.Block(f"op_w{sp}") as blk:
                    @blk.sync
                    def _(sync, sp=sp):
                        if sp > 0:
                            sync.wait_ge(SEM["pe"], P0 + PEB * sp)
                        for fc in range(HO):
                            sync.wait_ge(SEM["ld"], LD0 + (sp * HO + fc) * 16)
                            sync.dma_start(
                                out=awin[:, fc * SW:(fc + 1) * SW],
                                in_=bass.AP(D["at"], fc * (128 * S) + sp * SW,
                                            [[S, 128], [1, SW]])
                                ).then_inc(SEM["ld"], 16)
                        sync.wait_ge(SEM["ld"], LD0 + (sp + 1) * HO * 16)

                def bases(sp, oc):
                    return dict(
                        lw=L0 + (sp * DC + oc) * 16,
                        cast=CA0 + sp * DC + oc,
                        pe=P0 + PEB * sp + oc * NT2,
                        ev=E0 + PEB * sp + oc * NT2,
                        st=S0 + STB * sp + oc * NT2 * 16,
                        ld=LD0 + (sp + 1) * HO * 16,
                    )

                def O_sync(sync, sp, oc, par):
                    B = bases(sp, oc)
                    sync.wait_ge(SEM["cast"], B["cast"])
                    sync.wait_ge(SEM["lw"], B["lw"])
                    sync.dma_start(
                        out=wsf[:].rearrange("p (c o) -> p c o", c=HO),
                        in_=bass.AP(D["wo"], oc * (HO * 128 * 128),
                                    [[128, 128], [128 * 128, HO], [1, 128]]),
                    ).then_inc(SEM["lw"], 16)

                def O_vec(vec, sp, oc, par):
                    B = bases(sp, oc)
                    vec.wait_ge(SEM["lw"], B["lw"] + 16)
                    vec.wait_ge(SEM["pe"], B["pe"] - NT2)
                    vec.tensor_copy(wob[par][:], wsf[:]).then_inc(SEM["cast"], 1)
                    vec.drain()
                    for nt in range(NT2):
                        kp = nt % 2
                        vec.wait_ge(SEM["pe"], B["pe"] + nt + 1)
                        vec.wait_ge(SEM["st"], B["st"] + (nt - 2) * 16 + 16)
                        vec.tensor_copy(est[kp][:], ps_m[2 * par + kp][:])
                        vec.drain().then_inc(SEM["ev"], 1)

                def O_pe(te, sp, oc, par):
                    B = bases(sp, oc)
                    te.wait_ge(SEM["cast"], B["cast"] + 1)
                    te.wait_ge(SEM["ev"], B["ev"] - NT2)
                    te.wait_ge(SEM["ld"], B["ld"])
                    for nt in range(NT2):
                        if nt >= 2:
                            te.wait_ge(SEM["ev"], B["ev"] + nt - 1)
                        bank = ps_m[2 * par + nt % 2][:]
                        for fc in range(HO):
                            mm = te.matmul(bank,
                                           wob[par][:, fc * 128:(fc + 1) * 128],
                                           awin[:, fc * SW + nt * TOK:
                                                fc * SW + nt * TOK + TOK],
                                           start=(fc == 0), stop=(fc == HO - 1))
                        mm.then_inc(SEM["pe"], 1)

                def O_gp(gp, sp, oc, par):
                    B = bases(sp, oc)
                    for nt in range(NT2):
                        kp = nt % 2
                        gp.wait_ge(SEM["ev"], B["ev"] + nt + 1)
                        gp.wait_ge(SEM["st"], B["st"] + nt * 16)
                        col = sp * SW + nt * TOK
                        gp.dma_start(
                            out=bass.AP(D["opart"], oc * (128 * S) + col,
                                        [[S, 128], [1, TOK]]),
                            in_=est[kp][:]).then_inc(SEM["st"], 16)

                with nc.Block(f"op_m{sp}") as blk:
                    @blk.sync
                    def _(sync, sp=sp):
                        with sync.Fori(0, DC // 2) as it:
                            O_sync(sync, sp, it * 2, 0)
                            O_sync(sync, sp, it * 2 + 1, 1)

                    @blk.vector
                    def _(vec, sp=sp):
                        with vec.Fori(0, DC // 2) as it:
                            O_vec(vec, sp, it * 2, 0)
                            O_vec(vec, sp, it * 2 + 1, 1)

                    @blk.tensor
                    def _(te, sp=sp):
                        with te.Fori(0, DC // 2) as it:
                            O_pe(te, sp, it * 2, 0)
                            O_pe(te, sp, it * 2 + 1, 1)

                    @blk.gpsimd
                    def _(gp, sp=sp):
                        with gp.Fori(0, DC // 2) as it:
                            O_gp(gp, sp, it * 2, 0)
                            O_gp(gp, sp, it * 2 + 1, 1)
                        gp.wait_ge(SEM["st"], S0 + STB * (sp + 1))
                self.free_temps()

            C["ld"] += 2 * HO * 16
            C["lw"] += 2 * DC * 16
            C["cast"] += 2 * DC
            C["pe"] += 2 * PEB
            C["ev"] += 2 * PEB
            C["st"] += 2 * STB

            with nc.Block("op_cp") as blk:
                @blk.gpsimd
                def _(gp):
                    gp.wait_ge(SEM["st"], C["st"])
                    with gp.If_eq(self.rankreg["g"], 0):
                        gp.dma_start(out=bass.AP(D["sh_op"], 0, [[1, dm * S]]),
                                     in_=bass.AP(D["opart"], 0, [[1, dm * S]])
                                     ).then_inc(SEM["st"], 16)
                    with gp.Else():
                        gp.dma_start(out=bass.AP(D["sh_op"], dm * S,
                                                 [[1, dm * S]]),
                                     in_=bass.AP(D["opart"], 0, [[1, dm * S]])
                                     ).then_inc(SEM["st"], 16)
                    gp.wait_ge(SEM["st"], C["st"] + 16)
            C["st"] += 16
        self.last_store = (SEM["st"], C["st"])
        self.free_temps()

    def finish(self):
        self.es.close()
        return self.nc


def build(g, n_phases=99, solo=False):
    b = Build(g, n_phases, solo=solo)
    b.p_setup()
    b.ln_phase("m1", b.D["x_own"], True)
    b.ffn_phase("m2", b.D["wup"], None, b.D["wupb"], b.D["x_own"],
                b.D["x1"])
    b.ln_phase("m3", b.D["x1"], False, shared_dst=b.D["sh_lnx2"])
    b.flag_write(1)
    b.ln_phase("r1", b.D["xs0"], True, lnb_dst=b.D["lnb0"])
    b.ffn_phase("r2", None, b.D["wupb"], None, b.D["xs0"], b.D["x1s0"],
                lnb_src=b.D["lnb0"])
    b.ln_phase("r3", b.D["x1s0"], False, lnb_dst=b.D["lnb0"],
               shared_dst=b.D["sh_lnx20"])
    b.flag_write(2)
    b.flag_poll(1)
    b.qkv_phase()
    b.flag_poll(2)
    b.imp_phase()
    b.flag_write(3)
    b.flag_poll(3)
    b.rank_phase()
    b.zero_kv_phase()
    b.attn_phase()
    b.oproj_phase()
    b.flag_write(4)
    b.flag_poll(4)
    b.ln_phase("m8", b.D["x1"], True, preadd=True, x_save=b.D["x2"])
    b.ffn_phase("m9", None, b.D["wupb"], None, b.D["x2"], b.D["y"])
    b.finish()
    return b


def _rope_tables(S):
    inv = 1.0 / (10000.0 ** (np.arange(0, 128, 2, dtype=np.float64) / 128.0))
    fr = np.arange(S, dtype=np.float64)[:, None] * inv[None, :]
    emb = np.concatenate([fr, fr], axis=1)
    return (np.cos(emb).astype(np.float32).T.copy(),
            np.sin(emb).astype(np.float32).T.copy())


def prep_inputs(g, x, qw, kw, vw, ow, wup, gate_w, ebias, la, lb):
    dm, S, T, HO, DC = g.dm, g.S, g.T, g.HO, g.DC
    cosT, sinT = _rope_tables(S)
    wgate = np.ascontiguousarray(gate_w.T)
    gbias = np.ascontiguousarray(ebias.reshape(4, 1))
    wa = np.ascontiguousarray(la.reshape(32, dm).T)
    wb = np.ascontiguousarray(lb.transpose(0, 2, 1).reshape(32, dm))
    wupL = np.ascontiguousarray(
        wup.reshape(DC, 128, dm).transpose(0, 2, 1))
    ins = []
    for c in range(8):
        p, r = c // 2, c % 2
        x_own = np.ascontiguousarray(x[p, r * T:(r + 1) * T, :].T)
        xs0 = np.ascontiguousarray(x[0, r * T:(r + 1) * T, :].T)
        wqkv = np.empty((HO, 3, dm, 128), np.float32)
        for fc in range(HO):
            gh = r * HO + fc
            wqkv[fc, 0] = qw[gh * 128:(gh + 1) * 128, :].T
            wqkv[fc, 1] = kw[gh * 128:(gh + 1) * 128, :].T
            wqkv[fc, 2] = vw[gh * 128:(gh + 1) * 128, :].T
        wo = np.ascontiguousarray(
            ow.reshape(DC, 128, 2, HO * 128)[:, :, r, :].transpose(0, 2, 1))
        ins.append(dict(
            x_own=x_own, xs0=xs0, wqkv=wqkv, wo=wo, wup=wupL,
            wgate=wgate, gbias=gbias, wa=wa, wb=wb,
            cosT=cosT, sinT=sinT,
            rank=np.array([[r]], np.int32),
        ))
    return ins


_BUILD_CACHE = {}


def kernel(**inputs):
    g = FULL
    dm, S, T = g.dm, g.S, g.T
    x = np.asarray(inputs["L0_attn_hidden_states"], np.float32)
    ins = prep_inputs(
        g, x,
        np.asarray(inputs["L0_attn_q_w"], np.float32),
        np.asarray(inputs["L0_attn_k_w"], np.float32),
        np.asarray(inputs["L0_attn_v_w"], np.float32),
        np.asarray(inputs["L0_attn_o_w"], np.float32),
        np.asarray(inputs["L0_ffn_W_up"], np.float32),
        np.asarray(inputs["L0_ffn_gate_w"], np.float32),
        np.asarray(inputs["L0_ffn_expert_bias"], np.float32),
        np.asarray(inputs["L0_ffn_lora_A"], np.float32),
        np.asarray(inputs["L0_ffn_lora_B"], np.float32))
    if "nc" not in _BUILD_CACHE:
        _BUILD_CACHE["nc"] = build(g).nc
    nc = _BUILD_CACHE["nc"]
    res = run_bass_kernel_spmd(nc, ins, core_ids=list(range(8)))
    out = np.empty((4, S, dm), np.float32)
    for c in range(8):
        p, r = c // 2, c % 2
        out[p, r * T:(r + 1) * T, :] = \
            np.asarray(res.results[c]["y"], np.float32).T
    return out
